# version 12
# baseline (speedup 1.0000x reference)
import contextlib

import numpy as np
import ml_dtypes

import concourse.bass as bass
import concourse.mybir as mybir
from concourse.bass_utils import run_bass_kernel_spmd

F32 = mybir.dt.float32
BF16 = mybir.dt.bfloat16
U8 = mybir.dt.uint8
AF = mybir.ActivationFunctionType
OP = mybir.AluOpType

NCORES = 8
S = 8192
D = 4096
T = 1024
H = 32
DN = 128
DQ = 192
QSCALE = DQ ** -0.5
CSHIFT = 24.0
EPS = 1e-5
R0, R1 = 32, 8
RR = R0 + R1
NE = 3
NDT = D // 128
CF_W = NE * (R0 + R1)
NGRP = 11

DBG = []


def _host_fourier_mt():
    n = D
    j = np.arange(n)
    M = np.zeros((n, n), dtype=np.float64)
    M[:, 0] = 1.0 / 64.0
    f = np.arange(1, (n - 1) // 2 + 1)
    ang = 2.0 * np.pi * np.outer(j, f) / n
    M[:, 1:1 + 2 * len(f):2] = np.sqrt(2.0 / n) * np.cos(ang)
    M[:, 2:2 + 2 * len(f):2] = np.sqrt(2.0 / n) * np.sin(ang)
    M[:, n - 1] = ((-1.0) ** j) / 64.0
    return np.ascontiguousarray(0.25 * M.T).astype(ml_dtypes.bfloat16)


def _host_attn_masks(c):
    j = np.arange(128)[:, None]
    ff = np.arange(128)[None, :]
    m = np.zeros((128, 8 * 128), dtype=np.float32)
    for cc2 in range(8):
        m[:, cc2 * 128:(cc2 + 1) * 128] = (
            (j < ff) | ((j == ff) & (cc2 <= c))).astype(np.float32)
    return np.ascontiguousarray(m).astype(ml_dtypes.bfloat16)


ATT3 = {0: [], 1: []}
GRP3 = {0: [], 1: []}
PAIR_LAST_QK = []
_n = _g = _pg = 0
for _half in (0, 1):
    _g0 = _g
    for _hp in range(16):
        for _h2 in (0, 1):
            for _hsrc in range(_half + 1):
                for _cc2 in range(8):
                    ATT3[_half].append(dict(
                        n=_n, g=0, pg=_pg, hp=_hp, h2=_h2,
                        h=2 * _hp + _h2,
                        ha=_half * H + 2 * _hp + _h2, cc2=_cc2, hsrc=_hsrc,
                        diag=(_hsrc == _half),
                        gfirst=(_h2 == 0 and _cc2 == 0),
                        pfirst=(_hsrc == 0 and _h2 == 0 and _cc2 == 0),
                        hfirst=(_hsrc == 0 and _cc2 == 0),
                        hlast=(_hsrc == _half and _cc2 == 7)))
                    _n += 1
        for _hsrc in range(_half + 1):
            GRP3[_half].append(dict(g=_g, hp=_hp, hsrc=_hsrc))
            _g += 1
        PAIR_LAST_QK.append(_n - 1)
        _pg += 1
_GBYKEY = {}
for _half in (0, 1):
    for _grp in GRP3[_half]:
        _GBYKEY[(_half, _grp["hp"], _grp["hsrc"])] = _grp
for _half in (0, 1):
    for _st in ATT3[_half]:
        _grp = _GBYKEY[(_half, _st["hp"], _st["hsrc"])]
        _st["g"] = _grp["g"]
        _grp["last_use"] = _st["n"]
NSTEP = _n
H0_STEPS = len(ATT3[0])


RACE_DETECT = True


class _Bar:

    def __init__(self, nc, sem):
        self.nc = nc
        self.s = sem("pbar")
        self.k = 0

    def wait(self, eng):
        if self.k:
            eng.wait_ge(self.s, self.k)

    def close(self, waits):
        s = self.s
        with self.nc.Block() as blk:
            @blk.vector
            def _(v, waits=tuple(waits)):
                for s_, c_ in waits:
                    v.wait_ge(s_, c_)
                v.nop().then_inc(s, 1)
        self.k += 1


def build_program():
    nc = bass.Bass("TRN2", target_bir_lowering=False,
                   detect_race_conditions=RACE_DETECT)

    x0 = nc.dram_tensor("x0", [D, T], F32, kind="ExternalInput")
    wq = nc.dram_tensor("wq", [128, NDT * D], F32, kind="ExternalInput")
    wk = nc.dram_tensor("wk", [128, NDT * D], F32, kind="ExternalInput")
    wv = nc.dram_tensor("wv", [D, D], F32, kind="ExternalInput")
    wo = nc.dram_tensor("wo", [128, NDT * D], F32, kind="ExternalInput")
    wbt = nc.dram_tensor("wbt", [128, NDT * D], F32, kind="ExternalInput")
    bb = nc.dram_tensor("bb", [1, D], F32, kind="ExternalInput")
    enct = nc.dram_tensor("enct", [128, NDT * RR], F32, kind="ExternalInput")
    cfp = nc.dram_tensor("cfp", [128, NDT * CF_W], F32, kind="ExternalInput")
    mkp = nc.dram_tensor("mkp", [128, NDT * CF_W], U8, kind="ExternalInput")
    mtf = nc.dram_tensor("mtf", [D, D], BF16, kind="ExternalInput")
    amask = nc.dram_tensor("amask", [128, 8 * 128], BF16,
                           kind="ExternalInput")
    out = nc.dram_tensor("out", [D, T], F32, kind="ExternalOutput")

    x1 = nc.dram_tensor("x1", [D, T], F32, kind="Internal")
    x2 = nc.dram_tensor("x2", [D, T], F32, kind="Internal")
    qsp = nc.dram_tensor("qsp", [D, T], BF16, kind="Internal")
    src = [nc.dram_tensor(f"ccsrc{h}", [2 * D, 512], BF16, kind="Internal")
           for h in range(2)]
    dst = [nc.dram_tensor(f"ccdst{h}", [NCORES * 2 * D, 512], BF16,
                          kind="Internal") for h in range(2)]

    dram = {"x1": x1, "x2": x2}
    dbg_t = {n: nc.dram_tensor("tap_" + n, list(dram[n].shape), F32,
                               kind="ExternalOutput") for n in DBG if n in dram}
    if "xn1" in DBG:
        dbg_t["xn1"] = nc.dram_tensor("tap_xn1", [128, NDT * T], BF16,
                                      kind="ExternalOutput")
        dbg_t["asb"] = nc.dram_tensor("tap_asb", [RR, D], BF16,
                                      kind="ExternalOutput")
        dbg_t["esb"] = nc.dram_tensor("tap_esb", [RR, T], BF16,
                                      kind="ExternalOutput")

    onescol_b = nc.alloc_sbuf_tensor("onescol_b", [128, 1], BF16)
    onesrow_b = nc.alloc_sbuf_tensor("onesrow_b", [1, 512], BF16)
    onesrow_f = nc.alloc_sbuf_tensor("onesrow_f", [1, 128], F32)
    a_sb = nc.alloc_sbuf_tensor("a_sb", [RR, D], BF16)
    mask_b = nc.alloc_sbuf_tensor("mask_b", [128, 8 * 128], BF16)

    consts = []
    for v in (EPS, -0.5, -CSHIFT, QSCALE, 1.0 / D, 0.5, -1.0):
        t = nc.alloc_sbuf_tensor(f"cst{len(consts)}", [128, 1], F32)
        nc.const_aps.register(F32, float(v), t[:, :])
        consts.append((t, float(v)))

    ctx = contextlib.ExitStack()

    def sem(name):
        return ctx.enter_context(nc.semaphore(name))

    P = dict(onescol_b=onescol_b, onesrow_b=onesrow_b, onesrow_f=onesrow_f,
             a_sb=a_sb, mask_b=mask_b, consts=consts, dbg_t=dbg_t)

    bar = _Bar(nc, sem)
    P["bar"] = bar
    ccs = [sem("ccs0"), sem("ccs1")]
    with (
        nc.sbuf_tensor("xn", [128, NDT * T], BF16) as xn,
        nc.sbuf_tensor("stat2", [1, 4 * 512], F32) as stat2,
    ):
        w0 = _phase0(nc, sem, P, xn, x0, cfp, mkp, mtf, amask)
        bar.close(w0)
        if "xn1" in DBG:
            dt_ = sem("dbgxn")
            with nc.Block() as _b:
                @_b.gpsimd
                def _(g_):
                    bar.wait(g_)
                    g_.dma_start(dbg_t["xn1"].ap(), xn[:, :]).then_inc(dt_, 16)
                    g_.dma_start(dbg_t["asb"].ap(), P["a_sb"][0:RR, :]
                                 ).then_inc(dt_, 16)
            bar.close([(dt_, 32)])
        w1 = _phase_ffn(nc, sem, "f1", P, xn, wbt, bb, enct, x0, x1,
                        stat_out=stat2)
        bar.close(w1)
        w2 = _phase_ln(nc, sem, "l2", P, xn, x1, stat2)
        bar.close(w2)
        w3 = _phase_qkv(nc, sem, P, xn, wq, wk, wv, qsp, src, dst, ccs)
        bar.close(w3)
    with nc.sbuf_tensor("stat3", [1, 4 * 512], F32) as stat3:
        _phase_attn(nc, sem, P, qsp, dst, wo, x1, x2, ccs, stat_out=stat3)
        with (
            nc.sbuf_tensor("xn2", [128, NDT * T], BF16) as xn2,
        ):
            w4 = _phase_ln(nc, sem, "l3", P, xn2, x2, stat3)
            bar.close(w4)
            w5 = _phase_ffn(nc, sem, "f2", P, xn2, wbt, bb, enct, x2, out,
                            stat_out=None)
            bar.close(w5)
    if DBG:
        _phase_dbg(nc, sem, P, dram, dbg_t)

    ctx.close()
    return nc


def _phase0(nc, sem, P, xn, x0, cfp, mkp, mtf, amask):
    onescol_b, onesrow_f = P["onescol_b"], P["onesrow_f"]
    a_sb, mask_b = P["a_sb"], P["mask_b"]
    NT = 2 * NDT

    with (
        nc.sbuf_tensor("p0_cf", [128, NDT * CF_W], F32) as cf_st,
        nc.sbuf_tensor("p0_mk", [128, NDT * CF_W], U8) as mk_st,
        nc.sbuf_tensor("p0_ws", [128, NDT * RR], BF16) as ws_b,
        nc.sbuf_tensor("p0_wsf", [128, NDT * RR], F32) as wsf,
        nc.sbuf_tensor("p0_mt", [128, 2 * D], BF16) as mt_st,
        nc.sbuf_tensor("p0_xs", [128, 2 * 512], F32) as x_st,
        nc.sbuf_tensor("p0_xb", [128, 2 * 512], BF16) as xb_t,
        nc.sbuf_tensor("p0_sq", [128, 2 * 512], BF16) as sq_t,
        nc.sbuf_tensor("p0_xa", [128, 2 * 512], F32) as xa_st,
        nc.sbuf_tensor("p0_t1", [128, 512], F32) as t1_st,
        nc.sbuf_tensor("p0_rows", [1, 8 * 512], F32) as rows,
        nc.psum_tensor("p0_ps", [128, 8 * 512], F32) as ps,
    ):
        ld = sem("p0_ld")
        mtlp = [sem("p0_mtl0"), sem("p0_mtl1")]
        xlp = [sem("p0_xl0"), sem("p0_xl1")]
        xl2p = [sem("p0_xl2a"), sem("p0_xl2b")]
        wsr = sem("p0_wsr")
        peA = sem("p0_peA")
        aev = sem("p0_aev")
        stm = sem("p0_stm")
        bcr = sem("p0_bcr")
        xbr = sem("p0_xbr")
        sqr = sem("p0_sqr")
        rwa = sem("p0_rwa")
        rwd = sem("p0_rwd")
        dvap = sem("p0_dvap")

        def rowsl(i):
            return rows[:1, i * 512:(i + 1) * 512]

        bar = P["bar"]
        with nc.Block() as blk:
            @blk.sync
            def _(sync):
                bar.wait(sync)
                sync.dma_start(mask_b[:, :], amask.ap()).then_inc(ld, 16)
                sync.dma_start(cf_st[:, :], cfp.ap()).then_inc(ld, 16)
                sync.dma_start(mk_st[:, :], mkp.ap()).then_inc(ld, 16)
                for ot in range(NDT):
                    if ot >= 2:
                        sync.wait_ge(peA, ot - 1)
                    sync.dma_start(
                        mt_st[:, (ot % 2) * D:(ot % 2 + 1) * D],
                        bass.AP(mtf, ot * 128 * D, [[D, 128], [1, D]]),
                    ).then_inc(mtlp[ot % 2], 16)
                for g in range(NT):
                    sc, dt = g // NDT, g % NDT
                    if g >= 2:
                        sync.wait_ge(xbr, g - 1)
                        sync.wait_ge(sqr, g - 1)
                    sync.dma_start(
                        x_st[:, (g % 2) * 512:(g % 2 + 1) * 512],
                        bass.AP(x0, dt * 128 * T + sc * 512,
                                [[T, 128], [1, 512]])).then_inc(xlp[g % 2], 16)
                for g in range(NT):
                    sc, dt = g // NDT, g % NDT
                    if g >= 2:
                        sync.wait_ge(dvap, 2 + g - 1)
                    sync.dma_start(
                        xa_st[:, (g % 2) * 512:(g % 2 + 1) * 512],
                        bass.AP(x0, dt * 128 * T + sc * 512,
                                [[T, 128], [1, 512]])).then_inc(xl2p[g % 2], 16)

            @blk.vector
            def _(v):
                bar.wait(v)
                v.memset(onescol_b[:, :], 1.0)
                v.memset(P["onesrow_b"][:1, :], 1.0)
                v.memset(onesrow_f[:1, :], 1.0)
                for t, val in P["consts"]:
                    v.memset(t[:, :], val)
                v.wait_ge(ld, 16 * 3)
                def prod_ap(e, R, foff):
                    return bass.AP(cf_st, foff + e * R,
                                   [[NDT * CF_W, 128], [CF_W, NDT], [1, R]])

                def wsum_ap(R, roff):
                    return bass.AP(wsf, roff,
                                   [[NDT * RR, 128], [RR, NDT], [1, R]])

                v.tensor_tensor(cf_st[:, :], cf_st[:, :], mk_st[:, :],
                                OP.mult).then_inc(wsr, 1)
                v.wait_ge(wsr, 1)
                v.tensor_tensor(wsum_ap(R0, 0), prod_ap(0, R0, 0),
                                prod_ap(1, R0, 0), OP.add)
                v.tensor_tensor(wsum_ap(R1, R0), prod_ap(0, R1, NE * R0),
                                prod_ap(1, R1, NE * R0), OP.add
                                ).then_inc(wsr, 1)
                v.wait_ge(wsr, 2)
                v.tensor_tensor(wsum_ap(R0, 0), wsum_ap(R0, 0),
                                prod_ap(2, R0, 0), OP.add)
                v.tensor_tensor(wsum_ap(R1, R0), wsum_ap(R1, R0),
                                prod_ap(2, R1, NE * R0), OP.add
                                ).then_inc(wsr, 1)
                v.wait_ge(wsr, 3)
                v.tensor_copy(ws_b[:, :], wsf[:, :]).then_inc(wsr, NDT)
                for sc in range(2):
                    for dt in range(NDT):
                        g = sc * NDT + dt
                        v.wait_ge(xlp[g % 2], 16 * (g // 2 + 1))
                        if g >= 2:
                            v.wait_ge(stm, g - 1)
                        sl = slice((g % 2) * 512, (g % 2 + 1) * 512)
                        v.tensor_copy(xb_t[:, sl], x_st[:, sl]).then_inc(xbr, 1)
                    v.wait_ge(rwa, 2 * sc + 2)
                    v.tensor_tensor(rowsl(4 * sc + 2), rowsl(4 * sc),
                                    rowsl(4 * sc), OP.mult).then_inc(dvap, 1)
                    v.wait_ge(dvap, sc + 1)
                    v.tensor_tensor(rowsl(4 * sc + 2), rowsl(4 * sc + 1),
                                    rowsl(4 * sc + 2), OP.subtract
                                    ).then_inc(rwd, 1)
                for sc in range(2):
                    v.wait_ge(rwa, 5 + sc)
                    v.scalar_tensor_tensor(rowsl(4 * sc + 1), rowsl(4 * sc),
                                           -1.0, rowsl(4 * sc + 2),
                                           OP.mult, OP.mult).then_inc(rwd, 1)
                for sc in range(2):
                    for dt in range(NDT):
                        g = sc * NDT + dt
                        v.wait_ge(bcr, 2 * (sc + 1))
                        v.wait_ge(xl2p[g % 2], 16 * (g // 2 + 1))
                        xa = xa_st[:, (g % 2) * 512:(g % 2 + 1) * 512]
                        v.tensor_tensor(t1_st[:, :], xa,
                                        ps[:, (4 + 2 * sc) * 512:
                                           (5 + 2 * sc) * 512], OP.mult)
                        v.tensor_tensor(
                            xn[:, dt * T + sc * 512:dt * T + sc * 512 + 512],
                            t1_st[:, :],
                            ps[:, (5 + 2 * sc) * 512:(6 + 2 * sc) * 512],
                            OP.add).then_inc(dvap, 1)

            @blk.tensor
            def _(t):
                bar.wait(t)
                for ot in range(NDT):
                    t.wait_ge(mtlp[ot % 2], 16 * (ot // 2 + 1))
                    t.wait_ge(wsr, 3 + NDT)
                    for jc in range(8):
                        mm = t.matmul(
                            ps[0:RR, jc * 512:(jc + 1) * 512],
                            ws_b[:, ot * RR:(ot + 1) * RR],
                            mt_st[:, (ot % 2) * D + jc * 512:
                                  (ot % 2) * D + jc * 512 + 512],
                            start=(ot == 0), stop=(ot == NDT - 1))
                    mm.then_inc(peA, 1)
                t.wait_ge(aev, 8)
                for sc in range(2):
                    if sc == 1:
                        t.wait_ge(rwa, 2)
                    for dt in range(NDT):
                        g = sc * NDT + dt
                        t.wait_ge(xbr, g + 1)
                        t.wait_ge(sqr, g + 1)
                        sl = slice((g % 2) * 512, (g % 2 + 1) * 512)
                        t.matmul(ps[0:1, 0:512], onescol_b[:, :], xb_t[:, sl],
                                 start=(dt == 0), stop=(dt == NDT - 1))
                        t.matmul(ps[0:1, 512:1024], onescol_b[:, :],
                                 sq_t[:, sl], start=(dt == 0),
                                 stop=(dt == NDT - 1)).then_inc(stm, 1)
                for sc in range(2):
                    t.wait_ge(rwd, 2 + sc + 1)
                    t.matmul(ps[:, (4 + 2 * sc) * 512:(5 + 2 * sc) * 512],
                             onesrow_f[:1, :], rowsl(4 * sc + 2),
                             start=True, stop=True).then_inc(bcr, 1)
                    t.matmul(ps[:, (5 + 2 * sc) * 512:(6 + 2 * sc) * 512],
                             onesrow_f[:1, :], rowsl(4 * sc + 1),
                             start=True, stop=True).then_inc(bcr, 1)

            @blk.scalar
            def _(s):
                bar.wait(s)
                s.wait_ge(peA, NDT)
                for jc in range(8):
                    s.activation(a_sb[0:RR, jc * 512:(jc + 1) * 512],
                                 ps[0:RR, jc * 512:(jc + 1) * 512],
                                 AF.Copy).then_inc(aev, 1)
                for sc in range(2):
                    for dt in range(NDT):
                        g = sc * NDT + dt
                        s.wait_ge(xlp[g % 2], 16 * (g // 2 + 1))
                        if g >= 2:
                            s.wait_ge(stm, g - 1)
                        sl = slice((g % 2) * 512, (g % 2 + 1) * 512)
                        s.activation(sq_t[:, sl], x_st[:, sl], AF.Square
                                     ).then_inc(sqr, 1)
                    s.wait_ge(stm, NDT * (sc + 1))
                    s.activation(rowsl(4 * sc), ps[0:1, 0:512], AF.Copy,
                                 scale=1.0 / D).then_inc(rwa, 1)
                    s.activation(rowsl(4 * sc + 1), ps[0:1, 512:1024], AF.Copy,
                                 scale=1.0 / D).then_inc(rwa, 1)
                for sc in range(2):
                    s.wait_ge(rwd, sc + 1)
                    s.activation(rowsl(4 * sc + 3), rowsl(4 * sc + 2),
                                 AF.Ln, bias=EPS)
                    s.activation(rowsl(4 * sc + 2), rowsl(4 * sc + 3),
                                 AF.Exp, scale=-0.5).then_inc(rwa, 1)

    return [(dvap, 2 + 2 * NDT), (aev, 8)]


def _phase_ffn(nc, sem, tag, P, xn, wbt, bb, enct, x_in, x_out, stat_out):
    a_sb = P["a_sb"]
    onesrow_b, onescol_b = P["onesrow_b"], P["onescol_b"]
    NG = 2 * NDT

    with (
        nc.sbuf_tensor(f"{tag}_enc", [128, NDT * RR], F32) as enc_st,
        nc.sbuf_tensor(f"{tag}_encb", [128, NDT * RR], BF16) as enct_b,
        nc.sbuf_tensor(f"{tag}_bst", [1, D], F32) as b_st,
        nc.sbuf_tensor(f"{tag}_b05", [1, D], BF16) as b05_b,
        nc.sbuf_tensor(f"{tag}_esb", [RR, T], BF16) as e_sb,
        nc.sbuf_tensor(f"{tag}_wst", [128, 2 * D], F32) as wst,
        nc.sbuf_tensor(f"{tag}_wbb", [128, 2 * D], BF16) as wbb,
        nc.sbuf_tensor(f"{tag}_xi", [128, 2 * 512], F32) as xi,
        nc.sbuf_tensor(f"{tag}_xo", [128, 2 * 512], F32) as xo,
        nc.sbuf_tensor(f"{tag}_xb", [128, 2 * 512], BF16) as xb,
        nc.sbuf_tensor(f"{tag}_sq", [128, 2 * 512], BF16) as sq,
        nc.psum_tensor(f"{tag}_po", [128, 4 * 512], F32) as po,
        nc.psum_tensor(f"{tag}_pe", [128, 2 * 512], F32) as pev,
        nc.psum_tensor(f"{tag}_pst", [128, 2 * 512], F32) as pst,
    ):
        if "ffn_sems" not in P:
            P["ffn_sems"] = {k: sem("ffn_" + k) for k in (
                "wld0", "wld1", "wcast", "xld0", "xld1", "dva", "xbr", "sqr",
                "stc", "pem", "pee", "eev", "stx0", "stx1", "fin", "eld",
                "ini")}
            P["ffn_tot"] = {k: 0 for k in P["ffn_sems"]}
        SS = P["ffn_sems"]
        B = dict(P["ffn_tot"])
        wldp = [SS["wld0"], SS["wld1"]]
        xldp = [SS["xld0"], SS["xld1"]]
        stxp = [SS["stx0"], SS["stx1"]]
        wcast, dva, xbr, sqr = SS["wcast"], SS["dva"], SS["xbr"], SS["sqr"]
        stc, pem, pee, eev = SS["stc"], SS["pem"], SS["pee"], SS["eev"]
        fin, eld, ini = SS["fin"], SS["eld"], SS["ini"]
        bw = [B["wld0"], B["wld1"]]
        bx = [B["xld0"], B["xld1"]]
        bs = [B["stx0"], B["stx1"]]

        def stat_ps(ssc, which):
            t_ = pev if ssc == 0 else pst
            return t_[0:1, which * 512:(which + 1) * 512]

        bar = P["bar"]
        with nc.Block() as blk:
            @blk.sync
            def _(sync):
                bar.wait(sync)
                sync.dma_start(enc_st[:, :], enct.ap()).then_inc(eld, 16)
                sync.dma_start(b_st[:1, :], bb.ap()).then_inc(eld, 16)
                for ot in range(NDT):
                    if ot >= 2:
                        sync.wait_ge(wcast, B["wcast"] + ot - 1)
                    sync.dma_start(
                        wst[:, (ot % 2) * D:(ot % 2 + 1) * D],
                        bass.AP(wbt, ot * D, [[NDT * D, 128], [1, D]]),
                    ).then_inc(wldp[ot % 2], 16)
                    for g in (2 * ot, 2 * ot + 1):
                        if g >= 2:
                            sync.wait_ge(dva, B["dva"] + g - 1)
                        sync.dma_start(
                            xi[:, (g % 2) * 512:(g % 2 + 1) * 512],
                            bass.AP(x_in, (g // 2) * 128 * T + (g % 2) * 512,
                                    [[T, 128], [1, 512]])
                        ).then_inc(xldp[g % 2], 16)

            @blk.vector
            def _(v):
                bar.wait(v)
                v.wait_ge(eld, B["eld"] + 32)
                v.tensor_copy(enct_b[:, :], enc_st[:, :])
                v.tensor_scalar_mul(b05_b[:1, :], b_st[:1, :], 0.5
                                    ).then_inc(ini, 1)

                def wcast_one(ot):
                    v.wait_ge(wldp[ot % 2], bw[ot % 2] + 16 * (ot // 2 + 1))
                    if ot >= 2:
                        v.wait_ge(pem, B["pem"] + 2 * (ot - 1))
                    v.tensor_scalar_mul(
                        wbb[:, (ot % 2) * D:(ot % 2 + 1) * D],
                        wst[:, (ot % 2) * D:(ot % 2 + 1) * D], 0.5
                    ).then_inc(wcast, 1)

                wcast_one(0)
                wcast_one(1)
                for g in range(NG):
                    v.wait_ge(pem, B["pem"] + g + 1)
                    v.wait_ge(xldp[g % 2], bx[g % 2] + 16 * (g // 2 + 1))
                    if g >= 2:
                        v.wait_ge(stxp[g % 2], bs[g % 2] + 16 * (g // 2))
                        if stat_out is not None:
                            v.wait_ge(sqr, B["sqr"] + g - 1)
                    sl = slice((g % 2) * 512, (g % 2 + 1) * 512)
                    v.tensor_tensor(
                        xo[:, sl], po[:, (g % 4) * 512:(g % 4 + 1) * 512],
                        xi[:, sl], OP.add).then_inc(dva, 1)
                    if stat_out is not None:
                        if g >= 2:
                            v.wait_ge(stc, B["stc"] + g - 1)
                        v.tensor_copy(xb[:, sl], xo[:, sl]).then_inc(xbr, 1)
                    if g % 2 == 1 and g // 2 + 2 < NDT:
                        wcast_one(g // 2 + 2)

            @blk.tensor
            def _(t):
                bar.wait(t)
                t.wait_ge(ini, B["ini"] + 1)
                for sc in range(2):
                    for dt in range(NDT):
                        mm = t.matmul(
                            pev[0:RR, sc * 512:(sc + 1) * 512],
                            enct_b[:, dt * RR:(dt + 1) * RR],
                            xn[:, dt * T + sc * 512:dt * T + sc * 512 + 512],
                            start=(dt == 0), stop=(dt == NDT - 1))
                    mm.then_inc(pee, 1)
                t.wait_ge(eev, B["eev"] + 2)
                for g in range(NG):
                    ot, sc = g // 2, g % 2
                    t.wait_ge(wcast, B["wcast"] + ot + 1)
                    if g >= 4:
                        t.wait_ge(dva, B["dva"] + g - 3)
                    ob = slice((g % 4) * 512, (g % 4 + 1) * 512)
                    t.matmul(po[:, ob], b05_b[0:1, ot * 128:(ot + 1) * 128],
                             onesrow_b[:1, :], start=True, stop=False)
                    for dt in range(NDT):
                        t.matmul(po[:, ob],
                                 wbb[:, (ot % 2) * D + dt * 128:
                                     (ot % 2) * D + (dt + 1) * 128],
                                 xn[:, dt * T + sc * 512:dt * T + sc * 512 + 512],
                                 start=False, stop=False)
                    t.matmul(po[:, ob], a_sb[0:RR, ot * 128:(ot + 1) * 128],
                             e_sb[0:RR, sc * 512:(sc + 1) * 512],
                             start=False, stop=True).then_inc(pem, 1)
                    if stat_out is not None and g >= 2:
                        gs = g - 2
                        t.wait_ge(xbr, B["xbr"] + gs + 1)
                        t.wait_ge(sqr, B["sqr"] + gs + 1)
                        ssc = gs % 2
                        ww = gs // 2
                        sl = slice((gs % 2) * 512, (gs % 2 + 1) * 512)
                        t.matmul(stat_ps(ssc, 0), onescol_b[:, :], xb[:, sl],
                                 start=(ww == 0), stop=(ww == NDT - 1))
                        t.matmul(stat_ps(ssc, 1), onescol_b[:, :], sq[:, sl],
                                 start=(ww == 0), stop=(ww == NDT - 1)
                                 ).then_inc(stc, 1)
                if stat_out is not None:
                    for gs in range(NG - 2, NG):
                        t.wait_ge(xbr, B["xbr"] + gs + 1)
                        t.wait_ge(sqr, B["sqr"] + gs + 1)
                        ssc = gs % 2
                        ww = gs // 2
                        sl = slice((gs % 2) * 512, (gs % 2 + 1) * 512)
                        t.matmul(stat_ps(ssc, 0), onescol_b[:, :], xb[:, sl],
                                 start=(ww == 0), stop=(ww == NDT - 1))
                        t.matmul(stat_ps(ssc, 1), onescol_b[:, :], sq[:, sl],
                                 start=(ww == 0), stop=(ww == NDT - 1)
                                 ).then_inc(stc, 1)

            @blk.scalar
            def _(s):
                bar.wait(s)
                for sc in range(2):
                    s.wait_ge(pee, B["pee"] + sc + 1)
                    s.activation(e_sb[0:RR, sc * 512:(sc + 1) * 512],
                                 pev[0:RR, sc * 512:(sc + 1) * 512],
                                 AF.Copy).then_inc(eev, 1)
                if stat_out is not None:
                    for g in range(NG):
                        s.wait_ge(dva, B["dva"] + g + 1)
                        if g >= 2:
                            s.wait_ge(stc, B["stc"] + g - 1)
                        sl = slice((g % 2) * 512, (g % 2 + 1) * 512)
                        s.activation(sq[:, sl], xo[:, sl], AF.Square
                                     ).then_inc(sqr, 1)
                    s.wait_ge(stc, B["stc"] + NG)
                    for ssc in range(2):
                        s.activation(
                            stat_out[:1, 2 * ssc * 512:(2 * ssc + 1) * 512],
                            stat_ps(ssc, 0), AF.Copy, scale=1.0 / D
                        ).then_inc(fin, 1)
                        s.activation(
                            stat_out[:1, (2 * ssc + 1) * 512:
                                     (2 * ssc + 2) * 512],
                            stat_ps(ssc, 1), AF.Copy, scale=1.0 / D
                        ).then_inc(fin, 1)

            @blk.gpsimd
            def _(g_):
                bar.wait(g_)
                for g in range(NG):
                    ot, sc = g // 2, g % 2
                    g_.wait_ge(dva, B["dva"] + g + 1)
                    g_.dma_start(
                        bass.AP(x_out, ot * 128 * T + sc * 512,
                                [[T, 128], [1, 512]]),
                        xo[:, (g % 2) * 512:(g % 2 + 1) * 512],
                    ).then_inc(stxp[g % 2], 16)
                g_.wait_ge(stxp[0], bs[0] + 16 * (NG // 2))
                g_.wait_ge(stxp[1], bs[1] + 16 * (NG // 2))
                if stat_out is not None:
                    g_.wait_ge(fin, B["fin"] + 4)
                if tag == "f1" and "esb" in P.get("dbg_t", {}):
                    dt_ = sem("dbges")
                    g_.dma_start(P["dbg_t"]["esb"].ap(), e_sb[0:RR, :]
                                 ).then_inc(dt_, 16)
                    g_.wait_ge(dt_, 16)

    tot = P["ffn_tot"]
    for k, d in (("wld0", 256), ("wld1", 256), ("xld0", 512), ("xld1", 512),
                 ("wcast", 32), ("dva", 64), ("pem", 64), ("pee", 2),
                 ("eev", 2), ("eld", 32), ("ini", 1), ("stx0", 512),
                 ("stx1", 512)):
        tot[k] += d
    if stat_out is not None:
        for k, d in (("xbr", 64), ("sqr", 64), ("stc", 64), ("fin", 4)):
            tot[k] += d
    waits = [(stxp[0], tot["stx0"]), (stxp[1], tot["stx1"])]
    if stat_out is not None:
        waits.append((fin, tot["fin"]))
    return waits


def _phase_ln(nc, sem, tag, P, xn, x_src, stat):
    onesrow_f = P["onesrow_f"]
    NT = 2 * NDT

    with (
        nc.sbuf_tensor(f"{tag}_xa", [128, 2 * 512], F32) as xa,
        nc.sbuf_tensor(f"{tag}_t1", [128, 512], F32) as t1,
        nc.sbuf_tensor(f"{tag}_rows", [1, 4 * 512], F32) as rows,
        nc.psum_tensor(f"{tag}_bc", [128, 4 * 512], F32) as bc,
    ):
        if "ln_sems" not in P:
            P["ln_sems"] = {k: sem("ln_" + k) for k in (
                "xl0", "xl1", "rwa", "rwd", "bcr", "ap")}
            P["ln_tot"] = {k: 0 for k in P["ln_sems"]}
        SS = P["ln_sems"]
        B = dict(P["ln_tot"])
        xlp = [SS["xl0"], SS["xl1"]]
        rwa, rwd, bcr, ap = SS["rwa"], SS["rwd"], SS["bcr"], SS["ap"]
        bx = [B["xl0"], B["xl1"]]

        def rowsl(i):
            return rows[:1, i * 512:(i + 1) * 512]

        bar = P["bar"]
        with nc.Block() as blk:
            @blk.sync
            def _(sync):
                bar.wait(sync)
                for g in range(NT):
                    sc, dt = g // NDT, g % NDT
                    if g >= 2:
                        sync.wait_ge(ap, B["ap"] + 2 + g - 1)
                    sync.dma_start(
                        xa[:, (g % 2) * 512:(g % 2 + 1) * 512],
                        bass.AP(x_src, dt * 128 * T + sc * 512,
                                [[T, 128], [1, 512]])).then_inc(xlp[g % 2], 16)

            @blk.vector
            def _(v):
                bar.wait(v)
                for sc in range(2):
                    m_ = stat[:1, 2 * sc * 512:(2 * sc + 1) * 512]
                    e_ = stat[:1, (2 * sc + 1) * 512:(2 * sc + 2) * 512]
                    va = rowsl(2 * sc)
                    v.tensor_tensor(va, m_, m_, OP.mult).then_inc(ap, 1)
                    v.wait_ge(ap, B["ap"] + sc + 1)
                    v.tensor_tensor(va, e_, va, OP.subtract).then_inc(rwd, 1)
                for sc in range(2):
                    v.wait_ge(rwa, B["rwa"] + sc + 1)
                    m_ = stat[:1, 2 * sc * 512:(2 * sc + 1) * 512]
                    v.scalar_tensor_tensor(rowsl(2 * sc + 1), m_, -1.0,
                                           rowsl(2 * sc), OP.mult, OP.mult
                                           ).then_inc(rwd, 1)
                for sc in range(2):
                    for dt in range(NDT):
                        g = sc * NDT + dt
                        v.wait_ge(bcr, B["bcr"] + 2 * (sc + 1))
                        v.wait_ge(xlp[g % 2], bx[g % 2] + 16 * (g // 2 + 1))
                        sl = slice((g % 2) * 512, (g % 2 + 1) * 512)
                        v.tensor_tensor(t1[:, :], xa[:, sl],
                                        bc[:, 2 * sc * 512:(2 * sc + 1) * 512],
                                        OP.mult)
                        v.tensor_tensor(
                            xn[:, dt * T + sc * 512:dt * T + sc * 512 + 512],
                            t1[:, :],
                            bc[:, (2 * sc + 1) * 512:(2 * sc + 2) * 512],
                            OP.add).then_inc(ap, 1)

            @blk.scalar
            def _(s):
                bar.wait(s)
                for sc in range(2):
                    s.wait_ge(rwd, B["rwd"] + sc + 1)
                    tmp = rowsl(3)
                    s.activation(tmp, rowsl(2 * sc), AF.Ln, bias=EPS)
                    s.activation(rowsl(2 * sc), tmp, AF.Exp, scale=-0.5
                                 ).then_inc(rwa, 1)

            @blk.tensor
            def _(t):
                bar.wait(t)
                for sc in range(2):
                    t.wait_ge(rwd, B["rwd"] + 2 + sc + 1)
                    t.matmul(bc[:, 2 * sc * 512:(2 * sc + 1) * 512],
                             onesrow_f[:1, :], rowsl(2 * sc),
                             start=True, stop=True).then_inc(bcr, 1)
                    t.matmul(bc[:, (2 * sc + 1) * 512:(2 * sc + 2) * 512],
                             onesrow_f[:1, :], rowsl(2 * sc + 1),
                             start=True, stop=True).then_inc(bcr, 1)

    tot = P["ln_tot"]
    for k, d in (("xl0", 16 * 32), ("xl1", 16 * 32), ("rwa", 2), ("rwd", 4),
                 ("bcr", 4), ("ap", 2 + NT)):
        tot[k] += d
    return [(ap, tot["ap"])]


def _phase_qkv(nc, sem, P, xn, wq, wk, wv, qsp, src, dst, ccs):
    NV = 2 * 4 * NDT

    with (
        nc.sbuf_tensor("qk_wst", [128, 2 * D], F32) as wst,
        nc.sbuf_tensor("qk_wbb", [128, 2 * D], BF16) as wbb,
        nc.sbuf_tensor("qk_vw", [128, 4 * 1024], F32) as vw,
        nc.sbuf_tensor("qk_vwb", [128, 4 * 1024], BF16) as vwb,
        nc.sbuf_tensor("qk_st", [128, 4 * 512], BF16) as stg,
        nc.psum_tensor("qk_po", [128, 8 * 512], F32) as po,
    ):
        wldp = [sem("qk_wld0"), sem("qk_wld1")]
        wcast = sem("qk_wcast")
        vldp = [sem(f"qk_vld{j}") for j in range(4)]
        vcast = sem("qk_vcast")
        pei = sem("qk_pei")
        pem = sem("qk_pem")
        aev = sem("qk_aev")
        avv = sem("qk_avv")
        stkp = [sem(f"qk_stk{j}") for j in range(4)]
        stqp = [sem(f"qk_stq{j}") for j in range(4)]

        bar = P["bar"]
        with nc.Block() as blk:
            @blk.sync
            def _(sync):
                bar.wait(sync)
                for et in range(NDT):
                    if et >= 2:
                        sync.wait_ge(wcast, et - 1)
                    sync.dma_start(
                        wst[:, (et % 2) * D:(et % 2 + 1) * D],
                        bass.AP(wk, et * D, [[NDT * D, 128], [1, D]]),
                    ).then_inc(wldp[et % 2], 16)
                for i in range(NV):
                    dt = i % NDT
                    ev2 = (i // NDT) % 4
                    if i >= 4:
                        sync.wait_ge(vcast, i - 3)
                    sync.dma_start(
                        vw[:, (i % 4) * 1024:(i % 4 + 1) * 1024],
                        bass.AP(wv, dt * 128 * D + ev2 * 1024,
                                [[D, 128], [1, 1024]])
                    ).then_inc(vldp[i % 4], 16)
                for et in range(NDT):
                    if et >= 2:
                        sync.wait_ge(wcast, NDT + et - 1)
                    sync.dma_start(
                        wst[:, (et % 2) * D:(et % 2 + 1) * D],
                        bass.AP(wq, et * D, [[NDT * D, 128], [1, D]]),
                    ).then_inc(wldp[et % 2], 16)

            @blk.vector
            def _(v):
                bar.wait(v)
                for et in range(NDT):
                    v.wait_ge(wldp[et % 2], 16 * (et // 2 + 1))
                    if et >= 2:
                        v.wait_ge(pem, 2 * (et - 1))
                    v.tensor_copy(wbb[:, (et % 2) * D:(et % 2 + 1) * D],
                                  wst[:, (et % 2) * D:(et % 2 + 1) * D]
                                  ).then_inc(wcast, 1)
                for i in range(NV):
                    v.wait_ge(vldp[i % 4], 16 * (i // 4 + 1))
                    if i >= 4:
                        v.wait_ge(pei, i - 3)
                    v.tensor_copy(vwb[:, (i % 4) * 1024:(i % 4 + 1) * 1024],
                                  vw[:, (i % 4) * 1024:(i % 4 + 1) * 1024]
                                  ).then_inc(vcast, 1)
                for et in range(NDT):
                    v.wait_ge(wldp[et % 2], 16 * ((NDT + et) // 2 + 1))
                    if et >= 2:
                        v.wait_ge(pem, 64 + 2 * (et - 1))
                    v.tensor_copy(wbb[:, (et % 2) * D:(et % 2 + 1) * D],
                                  wst[:, (et % 2) * D:(et % 2 + 1) * D]
                                  ).then_inc(wcast, 1)

            @blk.tensor
            def _(t):
                bar.wait(t)
                for et in range(NDT):
                    t.wait_ge(wcast, et + 1)
                    for sc in range(2):
                        g = et * 2 + sc
                        if g >= 4:
                            t.wait_ge(aev, g - 3)
                        ob = slice((g % 4) * 512, (g % 4 + 1) * 512)
                        for dt in range(NDT):
                            mm = t.matmul(
                                po[:, ob],
                                wbb[:, (et % 2) * D + dt * 128:
                                    (et % 2) * D + (dt + 1) * 128],
                                xn[:, dt * T + sc * 512:dt * T + sc * 512 + 512],
                                start=(dt == 0), stop=(dt == NDT - 1))
                        mm.then_inc(pem, 1)
                t.wait_ge(aev, 64)
                for half in range(2):
                    for ev2 in range(4):
                        ge = half * 4 + ev2
                        for dt in range(NDT):
                            i = half * 4 * NDT + ev2 * NDT + dt
                            t.wait_ge(vcast, i + 1)
                            if dt == 0 and ge >= 1:
                                t.wait_ge(avv, 8 * ge)
                            for sub in range(8):
                                evh, st4 = sub // 4, sub % 4
                                mm = t.matmul(
                                    po[:, sub * 512:(sub + 1) * 512],
                                    xn[:, dt * T + (half * 4 + st4) * 128:
                                       dt * T + (half * 4 + st4 + 1) * 128],
                                    vwb[:, (i % 4) * 1024 + evh * 512:
                                        (i % 4) * 1024 + evh * 512 + 512],
                                    start=(dt == 0), stop=(dt == NDT - 1))
                            mm.then_inc(pei, 1)
                            del mm
                t.wait_ge(avv, 64)
                for et in range(NDT):
                    t.wait_ge(wcast, NDT + et + 1)
                    for sc in range(2):
                        g = 64 + et * 2 + sc
                        if g >= 68:
                            t.wait_ge(aev, g - 3)
                        ob = slice((g % 4) * 512, (g % 4 + 1) * 512)
                        for dt in range(NDT):
                            mm = t.matmul(
                                po[:, ob],
                                wbb[:, (et % 2) * D + dt * 128:
                                    (et % 2) * D + (dt + 1) * 128],
                                xn[:, dt * T + sc * 512:dt * T + sc * 512 + 512],
                                start=(dt == 0), stop=(dt == NDT - 1))
                        mm.then_inc(pem, 1)

            @blk.scalar
            def _(s):
                bar.wait(s)
                for g in range(64):
                    s.wait_ge(pem, g + 1)
                    if g >= 4:
                        s.wait_ge(stkp[g % 4], 16 * (g // 4))
                    s.activation(stg[:, (g % 4) * 512:(g % 4 + 1) * 512],
                                 po[:, (g % 4) * 512:(g % 4 + 1) * 512],
                                 AF.Copy).then_inc(aev, 1)
                for ge in range(8):
                    s.wait_ge(pei, (ge + 1) * NDT)
                    for sub in range(8):
                        j = 8 * ge + sub
                        if j >= 4:
                            s.wait_ge(stkp[j % 4], 16 * (16 + j // 4))
                        else:
                            s.wait_ge(stkp[j % 4], 16 * 16)
                        s.activation(stg[:, (j % 4) * 512:(j % 4 + 1) * 512],
                                     po[:, sub * 512:(sub + 1) * 512],
                                     AF.Copy).then_inc(avv, 1)
                for g in range(64):
                    s.wait_ge(pem, 64 + g + 1)
                    if g < 4:
                        s.wait_ge(stkp[g % 4], 16 * 32)
                    else:
                        s.wait_ge(stqp[g % 4], 16 * (g // 4))
                    s.activation(stg[:, (g % 4) * 512:(g % 4 + 1) * 512],
                                 po[:, (g % 4) * 512:(g % 4 + 1) * 512],
                                 AF.Copy, scale=QSCALE).then_inc(aev, 1)

            @blk.gpsimd
            def _(g_):
                bar.wait(g_)
                for g in range(64):
                    et, sc = g // 2, g % 2
                    g_.wait_ge(aev, g + 1)
                    g_.dma_start(
                        bass.AP(src[sc], et * 128 * 512,
                                [[512, 128], [1, 512]]),
                        stg[:, (g % 4) * 512:(g % 4 + 1) * 512],
                    ).then_inc(stkp[g % 4], 16)
                for ge in range(8):
                    half, ev2 = ge // 4, ge % 4
                    for sub in range(8):
                        j = 8 * ge + sub
                        evh, st4 = sub // 4, sub % 4
                        ev = ev2 * 2 + evh
                        g_.wait_ge(avv, j + 1)
                        g_.dma_start(
                            bass.AP(src[half],
                                    D * 512 + st4 * 128 * D + ev * 512,
                                    [[D, 128], [1, 512]]),
                            stg[:, (j % 4) * 512:(j % 4 + 1) * 512],
                        ).then_inc(stkp[j % 4], 16)
                    if ge == 3:
                        for s4 in range(4):
                            g_.wait_ge(stkp[s4], 16 * 24)
                        g_.collective_compute(
                            "AllGather", OP.bypass,
                            replica_groups=[list(range(NCORES))],
                            ins=[src[0].ap().opt()], outs=[dst[0].ap().opt()],
                        ).then_inc(ccs[0], 1)
                    if ge == 7:
                        for s4 in range(4):
                            g_.wait_ge(stkp[s4], 16 * 32)
                        g_.collective_compute(
                            "AllGather", OP.bypass,
                            replica_groups=[list(range(NCORES))],
                            ins=[src[1].ap().opt()], outs=[dst[1].ap().opt()],
                        ).then_inc(ccs[1], 1)
                for g in range(64):
                    et, sc = g // 2, g % 2
                    g_.wait_ge(aev, 64 + g + 1)
                    g_.dma_start(
                        bass.AP(qsp, et * 128 * T + sc * 512,
                                [[T, 128], [1, 512]]),
                        stg[:, (g % 4) * 512:(g % 4 + 1) * 512],
                    ).then_inc(stqp[g % 4], 16)
                for s4 in range(4):
                    g_.wait_ge(stqp[s4], 16 * 16)

    return ([(s_, 16 * 16) for s_ in stqp] + [(s_, 16 * 32) for s_ in stkp])


def _phase_attn(nc, sem, P, qsp, dst, wo, x1, x2, ccs, stat_out):
    bar = P["bar"]
    mask_b, onesrow_f, onescol_b = P["mask_b"], P["onesrow_f"], P["onescol_b"]
    NW = NDT
    RKB = 2 * D * 512

    qldp = [sem("at_qld0"), sem("at_qld1")]
    kvg = [sem("at_kvg0"), sem("at_kvg1")]
    ptsr = sem("at_ptsr")
    elld = sem("at_elld")
    scd = sem("at_scd")
    expd = sem("at_expd")
    elev = sem("at_elev")
    pvd = sem("at_pvd")
    pvev = sem("at_pvev")
    bcm = sem("at_bcm")
    nrm = sem("at_nrm")
    rcp = sem("at_rcp")
    wldp = [sem("at_wld0"), sem("at_wld1")]
    wcast = sem("at_wcast")
    pem = sem("at_pem")
    xldp = [sem("at_xld0"), sem("at_xld1")]
    dva = sem("at_dva")
    xbr = sem("at_xbr")
    sqr = sem("at_sqr")
    stc = sem("at_stc")
    fin = sem("at_fin")
    stxp = [sem("at_stx0"), sem("at_stx1")]
    msz = sem("at_msz")

    GRP_ALL = GRP3[0] + GRP3[1]

    for half in (0, 1):
        hsteps = ATT3[half]
        hgroups = GRP3[half]

        with nc.sbuf_tensor(f"at{half}_af", [128, H * 512], BF16) as af:
            with (
                nc.sbuf_tensor(f"at{half}_oa", [128, H * 512], BF16) as oa,
                nc.sbuf_tensor(f"at{half}_ell", [1, H * 512], BF16) as ell,
            ):
                with (
                    nc.sbuf_tensor(f"at{half}_q", [128, 2 * 1024], BF16
                                   ) as q_sb,
                    nc.sbuf_tensor(f"at{half}_k", [128, 2 * 8192], BF16) as kr,
                    nc.sbuf_tensor(f"at{half}_v", [128, 2 * 8192], BF16) as vr,
                    nc.sbuf_tensor(f"at{half}_p", [128, 2 * 2048], BF16) as pt,
                    nc.sbuf_tensor(f"at{half}_ps", [128, 2 * 512], BF16
                                   ) as ptsum,
                    nc.psum_tensor(f"at{half}_sc", [128, 6 * 512], F32) as psc,
                    nc.psum_tensor(f"at{half}_pv", [128, 512], F32) as ppv,
                    nc.psum_tensor(f"at{half}_el", [128, 512], F32) as pel,
                ):
                    with nc.Block() as blk:
                        @blk.sync
                        def _(sync, half=half):
                            bar.wait(sync)
                            for hp in range(16):
                                pg = half * 16 + hp
                                if pg >= 2:
                                    sync.wait_ge(
                                        scd, 4 * (PAIR_LAST_QK[pg - 2] + 1))
                                for h2 in (0, 1):
                                    sync.dma_start(
                                        q_sb[:, (pg % 2) * 1024 + h2 * 512:
                                             (pg % 2) * 1024 + h2 * 512 + 512],
                                        bass.AP(qsp,
                                                (2 * hp + h2) * 128 * T +
                                                half * 512,
                                                [[T, 128], [1, 512]]),
                                    ).then_inc(qldp[pg % 2], 16)

                        @blk.gpsimd
                        def _(g_, half=half, hgroups=hgroups):
                            bar.wait(g_)
                            g_.wait_ge(ccs[0], 1)
                            if half == 1:
                                g_.wait_ge(ccs[1], 1)
                            for grp in hgroups:
                                g, hp, hsrc = grp["g"], grp["hp"], grp["hsrc"]
                                if g >= 2:
                                    g_.wait_ge(
                                        pvd, GRP_ALL[g - 2]["last_use"] + 1)
                                sb = (g % 2) * 8192
                                for h2 in (0, 1):
                                    g_.dma_start(
                                        kr[:, sb + h2 * 4096:
                                           sb + h2 * 4096 + 4096],
                                        bass.AP(dst[hsrc],
                                                (2 * hp + h2) * 128 * 512,
                                                [[512, 128], [RKB, NCORES],
                                                 [1, 512]]),
                                    ).then_inc(kvg[g % 2], 16)
                                for s in range(4):
                                    g_.dma_start(
                                        bass.AP(vr, sb + s * 256,
                                                [[2 * 8192, 128],
                                                 [1024, 8], [1, 256]]),
                                        bass.AP(dst[hsrc],
                                                D * 512 + (hp // 2) * 512 +
                                                ((2 * hp) % 4) * 128 +
                                                s * 128 * D,
                                                [[D, 128], [RKB, 8],
                                                 [1, 256]]),
                                    ).then_inc(kvg[g % 2], 16)

                        @blk.tensor
                        def _(t, half=half, hsteps=hsteps):
                            bar.wait(t)
                            t.wait_ge(msz, half + 1)

                            def ell_pv(st):
                                n, ha, cc2 = st["n"], st["ha"], st["cc2"]
                                vb = ((st["g"] % 2) * 8192 + cc2 * 1024 +
                                      st["h2"] * 128)
                                if st["hfirst"] and ha >= 1:
                                    t.wait_ge(elev, ha)
                                t.wait_ge(ptsr, n + 1)
                                t.matmul(
                                    pel[0:1, 0:512],
                                    onescol_b[:, :],
                                    ptsum[:, (n % 2) * 512:(n % 2 + 1) * 512],
                                    start=st["hfirst"], stop=st["hlast"]
                                ).then_inc(elld, 1)
                                if st["hfirst"] and ha >= 1:
                                    t.wait_ge(pvev, ha)
                                for s in range(4):
                                    f0 = 128 * s if st["diag"] else 0
                                    mm2 = t.matmul(
                                        ppv[:, f0:512],
                                        vr[:, vb + s * 256:
                                           vb + s * 256 + 128],
                                        pt[:, (n % 2) * 2048 + s * 512 + f0:
                                           (n % 2) * 2048 + (s + 1) * 512],
                                        start=(st["hfirst"] and s == 0),
                                        stop=(st["hlast"] and s == 3))
                                mm2.then_inc(pvd, 1)
                                del mm2

                            for i, st in enumerate(hsteps):
                                n, cc2, h2 = st["n"], st["cc2"], st["h2"]
                                kb = ((st["g"] % 2) * 8192 + h2 * 4096 +
                                      cc2 * 512)
                                if st["gfirst"]:
                                    t.wait_ge(kvg[st["g"] % 2],
                                              96 * (st["g"] // 2 + 1))
                                if st["pfirst"]:
                                    t.wait_ge(qldp[st["pg"] % 2],
                                              32 * (st["pg"] // 2 + 1))
                                qb = (st["pg"] % 2) * 1024 + h2 * 512
                                for s in range(4):
                                    k = 4 * n + s
                                    if k >= 6:
                                        t.wait_ge(expd, k - 5)
                                    f0 = 128 * s if st["diag"] else 0
                                    kbank = (k % 6) * 512
                                    t.matmul(
                                        psc[:, kbank + f0:kbank + 512],
                                        kr[:, kb + s * 128:kb + s * 128 + 128],
                                        q_sb[:, qb + f0:qb + 512],
                                        start=True, stop=True
                                    ).then_inc(scd, 1)
                                if i >= 1:
                                    ell_pv(hsteps[i - 1])
                            ell_pv(hsteps[-1])

                        @blk.scalar
                        def _(s_, half=half, hsteps=hsteps):
                            bar.wait(s_)
                            for st in hsteps:
                                n = st["n"]
                                if n >= 2:
                                    s_.wait_ge(pvd, n - 1)
                                for s in range(4):
                                    k = 4 * n + s
                                    s_.wait_ge(scd, k + 1)
                                    f0 = 128 * s if st["diag"] else 0
                                    kbank = (k % 6) * 512
                                    s_.activation(
                                        pt[:, (n % 2) * 2048 + s * 512 + f0:
                                           (n % 2) * 2048 + (s + 1) * 512],
                                        psc[:, kbank + f0:kbank + 512],
                                        AF.Exp, bias=-CSHIFT
                                    ).then_inc(expd, 1)

                        @blk.vector
                        def _(v, half=half, hsteps=hsteps):
                            bar.wait(v)
                            v.memset(psc[:, :], 0.0)
                            v.memset(pel[:, :], 0.0)
                            v.nop().then_inc(msz, 1)
                            for st in hsteps:
                                n, cc2, h = st["n"], st["cc2"], st["h"]
                                ha = st["ha"]
                                base = (n % 2) * 2048
                                b2 = (n % 2) * 512
                                v.wait_ge(expd, 4 * (n + 1))
                                if st["diag"]:
                                    for s in range(4):
                                        f0 = 128 * s
                                        sl_ = pt[:, base + s * 512 + f0:
                                                 base + s * 512 + f0 + 128]
                                        ins = v.tensor_tensor(
                                            sl_, sl_,
                                            mask_b[:, cc2 * 128:
                                                   (cc2 + 1) * 128],
                                            OP.mult)
                                    del ins
                                if n >= 2:
                                    v.wait_ge(elld, n - 1)
                                if st["diag"]:
                                    v.tensor_copy(ptsum[:, b2:b2 + 512],
                                                  pt[:, base:base + 512])
                                    for s in range(1, 4):
                                        f0 = 128 * s
                                        ins = v.tensor_tensor(
                                            ptsum[:, b2 + f0:b2 + 512],
                                            ptsum[:, b2 + f0:b2 + 512],
                                            pt[:, base + s * 512 + f0:
                                               base + (s + 1) * 512],
                                            OP.add)
                                    ins.then_inc(ptsr, 1)
                                    del ins
                                else:
                                    ssl = ptsum[:, b2:b2 + 512]
                                    v.tensor_tensor(ssl,
                                                    pt[:, base:base + 512],
                                                    pt[:, base + 512:
                                                       base + 1024], OP.add)
                                    v.tensor_tensor(ssl, ssl,
                                                    pt[:, base + 1024:
                                                       base + 1536], OP.add)
                                    v.tensor_tensor(ssl, ssl,
                                                    pt[:, base + 1536:
                                                       base + 2048], OP.add
                                                    ).then_inc(ptsr, 1)
                                if st["hlast"]:
                                    v.wait_ge(pvd, n + 1)
                                    v.tensor_copy(
                                        oa[:, h * 512:(h + 1) * 512],
                                        ppv[:, 0:512]).then_inc(pvev, 1)
                                    v.tensor_copy(
                                        ell[0:1, h * 512:(h + 1) * 512],
                                        pel[0:1, 0:512]
                                    ).then_inc(elev, 1)

                bar.close([(pvev, H * (half + 1)), (elev, H * (half + 1))])

                with (
                    nc.sbuf_tensor(f"at{half}_inv", [1, 2 * 512], F32) as inv,
                    nc.psum_tensor(f"at{half}_bc", [128, 2 * 512], F32) as pbc,
                ):
                    with nc.Block() as blk:
                        @blk.vector
                        def _(v, half=half):
                            bar.wait(v)
                            for hh in range(H):
                                idx = half * H + hh
                                if hh >= 2:
                                    v.wait_ge(bcm, idx - 1)
                                v.reciprocal(
                                    inv[:1, (hh % 2) * 512:(hh % 2 + 1) * 512],
                                    ell[0:1, hh * 512:(hh + 1) * 512]
                                ).then_inc(rcp, 1)
                                v.wait_ge(bcm, idx + 1)
                                v.tensor_tensor(
                                    af[:, hh * 512:(hh + 1) * 512],
                                    oa[:, hh * 512:(hh + 1) * 512],
                                    pbc[:, (hh % 2) * 512:(hh % 2 + 1) * 512],
                                    OP.mult).then_inc(nrm, 1)

                        @blk.tensor
                        def _(t, half=half):
                            bar.wait(t)
                            for hh in range(H):
                                idx = half * H + hh
                                t.wait_ge(rcp, idx + 1)
                                if hh >= 2:
                                    t.wait_ge(nrm, idx - 1)
                                t.matmul(
                                    pbc[:, (hh % 2) * 512:(hh % 2 + 1) * 512],
                                    onesrow_f[:1, :],
                                    inv[:1, (hh % 2) * 512:(hh % 2 + 1) * 512],
                                    start=True, stop=True).then_inc(bcm, 1)

            bar.close([(nrm, H * (half + 1))])

            with (
                nc.sbuf_tensor(f"at{half}_wst", [128, 2 * D], F32) as wst,
                nc.sbuf_tensor(f"at{half}_wbb", [128, 2 * D], BF16) as wbb,
                nc.sbuf_tensor(f"at{half}_xi", [128, 2 * 512], F32) as xi,
                nc.sbuf_tensor(f"at{half}_xo", [128, 2 * 512], F32) as xo,
                nc.sbuf_tensor(f"at{half}_xb", [128, 2 * 512], BF16) as xb,
                nc.sbuf_tensor(f"at{half}_sq", [128, 2 * 512], BF16) as sq,
                nc.psum_tensor(f"at{half}_wo", [128, 2 * 512], F32) as pwo,
                nc.psum_tensor(f"at{half}_st", [128, 2 * 512], F32) as pstt,
            ):
                with nc.Block() as blk:
                    @blk.sync
                    def _(sync, half=half):
                        bar.wait(sync)
                        for ot in range(NDT):
                            w = half * NW + ot
                            if ot >= 2:
                                sync.wait_ge(wcast, w - 1)
                            sync.dma_start(
                                wst[:, (ot % 2) * D:(ot % 2 + 1) * D],
                                bass.AP(wo, ot * D,
                                        [[NDT * D, 128], [1, D]]),
                            ).then_inc(wldp[w % 2], 16)
                            if ot >= 2:
                                sync.wait_ge(dva, w - 1)
                            sync.dma_start(
                                xi[:, (ot % 2) * 512:(ot % 2 + 1) * 512],
                                bass.AP(x1, ot * 128 * T + half * 512,
                                        [[T, 128], [1, 512]]),
                            ).then_inc(xldp[w % 2], 16)

                    @blk.vector
                    def _(v, half=half):
                        bar.wait(v)

                        def wcast_one(ot):
                            w = half * NW + ot
                            v.wait_ge(wldp[w % 2], 16 * (w // 2 + 1))
                            if ot >= 2:
                                v.wait_ge(pem, w - 1)
                            v.tensor_copy(
                                wbb[:, (ot % 2) * D:(ot % 2 + 1) * D],
                                wst[:, (ot % 2) * D:(ot % 2 + 1) * D]
                            ).then_inc(wcast, 1)

                        wcast_one(0)
                        wcast_one(1)
                        for ot in range(NDT):
                            w = half * NW + ot
                            v.wait_ge(pem, w + 1)
                            v.wait_ge(xldp[w % 2], 16 * (w // 2 + 1))
                            if ot >= 2:
                                v.wait_ge(stxp[w % 2], 16 * (w // 2))
                                if stat_out is not None:
                                    v.wait_ge(sqr, w - 1)
                            sl = slice((ot % 2) * 512, (ot % 2 + 1) * 512)
                            v.tensor_tensor(
                                xo[:, sl],
                                pwo[:, (ot % 2) * 512:(ot % 2 + 1) * 512],
                                xi[:, sl], OP.add).then_inc(dva, 1)
                            if stat_out is not None:
                                if w >= 2:
                                    v.wait_ge(stc, w - 1)
                                v.tensor_copy(xb[:, sl], xo[:, sl]
                                              ).then_inc(xbr, 1)
                            if ot + 2 < NDT:
                                wcast_one(ot + 2)

                    @blk.tensor
                    def _(t, half=half):
                        bar.wait(t)
                        for ot in range(NDT):
                            w = half * NW + ot
                            t.wait_ge(wcast, w + 1)
                            if ot >= 2:
                                t.wait_ge(dva, w - 1)
                            ob = slice((ot % 2) * 512, (ot % 2 + 1) * 512)
                            for hh in range(H):
                                mm = t.matmul(
                                    pwo[:, ob],
                                    wbb[:, (ot % 2) * D + hh * 128:
                                        (ot % 2) * D + (hh + 1) * 128],
                                    af[:, hh * 512:(hh + 1) * 512],
                                    start=(hh == 0), stop=(hh == H - 1))
                            mm.then_inc(pem, 1)
                            del mm
                            if stat_out is not None and ot >= 2:
                                ws = ot - 2
                                w2 = half * NW + ws
                                t.wait_ge(xbr, w2 + 1)
                                t.wait_ge(sqr, w2 + 1)
                                sl = slice((ws % 2) * 512, (ws % 2 + 1) * 512)
                                t.matmul(pstt[0:1, 0:512], onescol_b[:, :],
                                         xb[:, sl], start=(ws == 0),
                                         stop=(ws == NDT - 1))
                                t.matmul(pstt[0:1, 512:1024], onescol_b[:, :],
                                         sq[:, sl], start=(ws == 0),
                                         stop=(ws == NDT - 1)).then_inc(stc, 1)
                        if stat_out is not None:
                            for ws in range(NDT - 2, NDT):
                                w2 = half * NW + ws
                                t.wait_ge(xbr, w2 + 1)
                                t.wait_ge(sqr, w2 + 1)
                                sl = slice((ws % 2) * 512, (ws % 2 + 1) * 512)
                                t.matmul(pstt[0:1, 0:512], onescol_b[:, :],
                                         xb[:, sl], start=(ws == 0),
                                         stop=(ws == NDT - 1))
                                t.matmul(pstt[0:1, 512:1024], onescol_b[:, :],
                                         sq[:, sl], start=(ws == 0),
                                         stop=(ws == NDT - 1)).then_inc(stc, 1)

                    @blk.scalar
                    def _(s_, half=half):
                        bar.wait(s_)
                        if stat_out is not None:
                            for ot in range(NDT):
                                w = half * NW + ot
                                s_.wait_ge(dva, w + 1)
                                if w >= 2:
                                    s_.wait_ge(stc, w - 1)
                                sl = slice((ot % 2) * 512, (ot % 2 + 1) * 512)
                                s_.activation(sq[:, sl], xo[:, sl], AF.Square
                                              ).then_inc(sqr, 1)
                            s_.wait_ge(stc, (half + 1) * NW)
                            s_.activation(
                                stat_out[:1, 2 * half * 512:
                                         (2 * half + 1) * 512],
                                pstt[0:1, 0:512], AF.Copy, scale=1.0 / D
                            ).then_inc(fin, 1)
                            s_.activation(
                                stat_out[:1, (2 * half + 1) * 512:
                                         (2 * half + 2) * 512],
                                pstt[0:1, 512:1024], AF.Copy, scale=1.0 / D
                            ).then_inc(fin, 1)

                    @blk.gpsimd
                    def _(g_, half=half):
                        bar.wait(g_)
                        for ot in range(NDT):
                            w = half * NW + ot
                            g_.wait_ge(dva, w + 1)
                            g_.dma_start(
                                bass.AP(x2, ot * 128 * T + half * 512,
                                        [[T, 128], [1, 512]]),
                                xo[:, (ot % 2) * 512:(ot % 2 + 1) * 512],
                            ).then_inc(stxp[w % 2], 16)
                        g_.wait_ge(stxp[0], 16 * (half + 1) * (NW // 2))
                        g_.wait_ge(stxp[1], 16 * (half + 1) * (NW // 2))

            bar.close([(stxp[0], 16 * (half + 1) * (NW // 2)),
                       (stxp[1], 16 * (half + 1) * (NW // 2)),
                       (fin, 2 * (half + 1))])


def _phase_dbg(nc, sem, P, dram, dbg_t):
    with nc.sbuf_tensor("dbg_st", [128, 512], F32) as stg:
        dl = sem("dbg_l")
        ds_ = sem("dbg_s")
        with nc.Block() as blk:
            @blk.gpsimd
            def _(g_):
                P["bar"].wait(g_)
                cnt = 0
                for name, t in dram.items():
                    if name not in dbg_t:
                        continue
                    tap = dbg_t[name]
                    rows, cols = t.shape
                    for rt in range(rows // 128):
                        for ct in range(cols // 512):
                            g_.dma_start(
                                stg[:, :],
                                bass.AP(t, rt * 128 * cols + ct * 512,
                                        [[cols, 128], [1, 512]]),
                            ).then_inc(dl, 16)
                            cnt += 1
                            g_.wait_ge(dl, 16 * cnt)
                            g_.dma_start(
                                bass.AP(tap, rt * 128 * cols + ct * 512,
                                        [[cols, 128], [1, 512]]),
                                stg[:, :]).then_inc(ds_, 16)
                            g_.wait_ge(ds_, 16 * cnt)


_NC_CACHE = {}


def _get_program():
    key = tuple(DBG)
    if key not in _NC_CACHE:
        _NC_CACHE[key] = build_program()
    return _NC_CACHE[key]


def kernel(L0_attn_hidden_states, L0_attn_wq, L0_attn_wkv, L0_attn_wo,
           L0_ffn_W_base, L0_ffn_b_base, L0_ffn_enc0, L0_ffn_enc1,
           L0_ffn_coefs0, L0_ffn_coefs1, L0_ffn_mask0, L0_ffn_mask1,
           _collect_raw=False):
    x = np.asarray(L0_attn_hidden_states)[0]

    def _tile_w(W):
        nd, ne = W.shape[0] // 128, W.shape[1] // 128
        return np.ascontiguousarray(
            W.reshape(nd, 128, ne, 128).transpose(1, 2, 0, 3)
            .reshape(128, ne * nd * 128))

    wq_full = np.asarray(L0_attn_wq)
    wq_h = _tile_w(np.ascontiguousarray(
        wq_full.reshape(D, H, DQ)[:, :, :DN].reshape(D, H * DN)))
    wkv_full = np.asarray(L0_attn_wkv)
    wk_h = _tile_w(np.ascontiguousarray(wkv_full[:, :D]))
    wv_ = np.ascontiguousarray(wkv_full[:, D:2 * D])
    wo_h = _tile_w(np.asarray(L0_attn_wo))
    wbt_h = _tile_w(np.ascontiguousarray(np.asarray(L0_ffn_W_base).T))
    bb_ = np.ascontiguousarray(np.asarray(L0_ffn_b_base).reshape(1, D))
    enct_full = np.concatenate(
        [np.asarray(L0_ffn_enc0), np.asarray(L0_ffn_enc1)], 0).T
    enct_h = np.ascontiguousarray(
        enct_full.reshape(NDT, 128, RR).transpose(1, 0, 2).reshape(128, -1))
    cfp_ = np.empty((128, NDT, CF_W), dtype=np.float32)
    mkp_ = np.empty((128, NDT, CF_W), dtype=np.uint8)
    cfp_[:, :, :NE * R0] = np.asarray(L0_ffn_coefs0).reshape(
        NE, NDT, 128, R0).transpose(2, 1, 0, 3).reshape(128, NDT, NE * R0)
    cfp_[:, :, NE * R0:] = np.asarray(L0_ffn_coefs1).reshape(
        NE, NDT, 128, R1).transpose(2, 1, 0, 3).reshape(128, NDT, NE * R1)
    mkp_[:, :, :NE * R0] = np.asarray(L0_ffn_mask0).astype(np.uint8).reshape(
        NE, NDT, 128, R0).transpose(2, 1, 0, 3).reshape(128, NDT, NE * R0)
    mkp_[:, :, NE * R0:] = np.asarray(L0_ffn_mask1).astype(np.uint8).reshape(
        NE, NDT, 128, R1).transpose(2, 1, 0, 3).reshape(128, NDT, NE * R1)
    cfp_ = np.ascontiguousarray(cfp_.reshape(128, NDT * CF_W))
    mkp_ = np.ascontiguousarray(mkp_.reshape(128, NDT * CF_W))
    mtf_ = _host_fourier_mt()

    in_maps = []
    for c in range(NCORES):
        in_maps.append({
            "x0": np.ascontiguousarray(x[c::8, :].T),
            "wq": wq_h, "wk": wk_h, "wv": wv_, "wo": wo_h, "wbt": wbt_h,
            "bb": bb_, "enct": enct_h, "cfp": cfp_, "mkp": mkp_,
            "mtf": mtf_, "amask": _host_attn_masks(c),
        })
    nc = _get_program()
    res = run_bass_kernel_spmd(nc, in_maps, core_ids=list(range(NCORES)))
    if _collect_raw:
        return res
    full = np.empty((S, D), dtype=np.float32)
    for c in range(NCORES):
        full[c::8, :] = res.results[c]["out"].T
    return full.reshape(1, S, D)


# version 13 (final)
# speedup vs baseline: 1.0099x; 1.0099x over previous
import contextlib

import numpy as np
import ml_dtypes

import concourse.bass as bass
import concourse.mybir as mybir
from concourse.bass_utils import run_bass_kernel_spmd

F32 = mybir.dt.float32
BF16 = mybir.dt.bfloat16
U8 = mybir.dt.uint8
AF = mybir.ActivationFunctionType
OP = mybir.AluOpType

NCORES = 8
S = 8192
D = 4096
T = 1024
H = 32
DN = 128
DQ = 192
QSCALE = DQ ** -0.5
CSHIFT = 24.0
EPS = 1e-5
R0, R1 = 32, 8
RR = R0 + R1
NE = 3
NDT = D // 128
CF_W = NE * (R0 + R1)
NGRP = 11

DBG = []


def _host_fourier_mt():
    n = D
    j = np.arange(n)
    M = np.zeros((n, n), dtype=np.float64)
    M[:, 0] = 1.0 / 64.0
    f = np.arange(1, (n - 1) // 2 + 1)
    ang = 2.0 * np.pi * np.outer(j, f) / n
    M[:, 1:1 + 2 * len(f):2] = np.sqrt(2.0 / n) * np.cos(ang)
    M[:, 2:2 + 2 * len(f):2] = np.sqrt(2.0 / n) * np.sin(ang)
    M[:, n - 1] = ((-1.0) ** j) / 64.0
    return np.ascontiguousarray(0.25 * M.T).astype(ml_dtypes.bfloat16)


def _host_attn_masks(c):
    j = np.arange(128)[:, None]
    ff = np.arange(128)[None, :]
    m = np.zeros((128, 8 * 128), dtype=np.float32)
    for cc2 in range(8):
        m[:, cc2 * 128:(cc2 + 1) * 128] = (
            (j < ff) | ((j == ff) & (cc2 <= c))).astype(np.float32)
    return np.ascontiguousarray(m).astype(ml_dtypes.bfloat16)


ATT3 = {0: [], 1: []}
GRP3 = {0: [], 1: []}
PAIR_LAST_QK = []
_n = _g = _pg = 0
for _half in (0, 1):
    _g0 = _g
    for _hp in range(16):
        for _h2 in (0, 1):
            for _hsrc in range(_half + 1):
                for _cc2 in range(8):
                    ATT3[_half].append(dict(
                        n=_n, g=0, pg=_pg, hp=_hp, h2=_h2,
                        h=2 * _hp + _h2,
                        ha=_half * H + 2 * _hp + _h2, cc2=_cc2, hsrc=_hsrc,
                        diag=(_hsrc == _half),
                        gfirst=(_h2 == 0 and _cc2 == 0),
                        pfirst=(_hsrc == 0 and _h2 == 0 and _cc2 == 0),
                        hfirst=(_hsrc == 0 and _cc2 == 0),
                        hlast=(_hsrc == _half and _cc2 == 7)))
                    _n += 1
        for _hsrc in range(_half + 1):
            GRP3[_half].append(dict(g=_g, hp=_hp, hsrc=_hsrc))
            _g += 1
        PAIR_LAST_QK.append(_n - 1)
        _pg += 1
_GBYKEY = {}
for _half in (0, 1):
    for _grp in GRP3[_half]:
        _GBYKEY[(_half, _grp["hp"], _grp["hsrc"])] = _grp
for _half in (0, 1):
    for _st in ATT3[_half]:
        _grp = _GBYKEY[(_half, _st["hp"], _st["hsrc"])]
        _st["g"] = _grp["g"]
        _grp["last_use"] = _st["n"]
NSTEP = _n
H0_STEPS = len(ATT3[0])


RACE_DETECT = True


class _Bar:

    def __init__(self, nc, sem):
        self.nc = nc
        self.s = sem("pbar")
        self.k = 0

    def wait(self, eng):
        if self.k:
            eng.wait_ge(self.s, self.k)

    def close(self, waits):
        s = self.s
        with self.nc.Block() as blk:
            @blk.vector
            def _(v, waits=tuple(waits)):
                for s_, c_ in waits:
                    v.wait_ge(s_, c_)
                v.nop().then_inc(s, 1)
        self.k += 1


def build_program():
    nc = bass.Bass("TRN2", target_bir_lowering=False,
                   detect_race_conditions=RACE_DETECT)

    x0 = nc.dram_tensor("x0", [D, T], F32, kind="ExternalInput")
    wq = nc.dram_tensor("wq", [128, NDT * D], F32, kind="ExternalInput")
    wk = nc.dram_tensor("wk", [128, NDT * D], F32, kind="ExternalInput")
    wv = nc.dram_tensor("wv", [D, D], F32, kind="ExternalInput")
    wo = nc.dram_tensor("wo", [128, NDT * D], F32, kind="ExternalInput")
    wbt = nc.dram_tensor("wbt", [128, NDT * D], F32, kind="ExternalInput")
    bb = nc.dram_tensor("bb", [1, D], F32, kind="ExternalInput")
    enct = nc.dram_tensor("enct", [128, NDT * RR], F32, kind="ExternalInput")
    cfp = nc.dram_tensor("cfp", [128, NDT * CF_W], F32, kind="ExternalInput")
    mkp = nc.dram_tensor("mkp", [128, NDT * CF_W], U8, kind="ExternalInput")
    mtf = nc.dram_tensor("mtf", [D, D], BF16, kind="ExternalInput")
    amask = nc.dram_tensor("amask", [128, 8 * 128], BF16,
                           kind="ExternalInput")
    out = nc.dram_tensor("out", [D, T], F32, kind="ExternalOutput")

    x1 = nc.dram_tensor("x1", [D, T], F32, kind="Internal")
    x2 = nc.dram_tensor("x2", [D, T], F32, kind="Internal")
    qsp = nc.dram_tensor("qsp", [D, T], BF16, kind="Internal")
    src = [nc.dram_tensor(f"ccsrc{h}", [2 * D, 512], BF16, kind="Internal")
           for h in range(2)]
    dst = [nc.dram_tensor(f"ccdst{h}", [NCORES * 2 * D, 512], BF16,
                          kind="Internal") for h in range(2)]

    dram = {"x1": x1, "x2": x2}
    dbg_t = {n: nc.dram_tensor("tap_" + n, list(dram[n].shape), F32,
                               kind="ExternalOutput") for n in DBG if n in dram}
    if "xn1" in DBG:
        dbg_t["xn1"] = nc.dram_tensor("tap_xn1", [128, NDT * T], BF16,
                                      kind="ExternalOutput")
        dbg_t["asb"] = nc.dram_tensor("tap_asb", [RR, D], BF16,
                                      kind="ExternalOutput")
        dbg_t["esb"] = nc.dram_tensor("tap_esb", [RR, T], BF16,
                                      kind="ExternalOutput")

    onescol_b = nc.alloc_sbuf_tensor("onescol_b", [128, 1], BF16)
    onesrow_b = nc.alloc_sbuf_tensor("onesrow_b", [1, 512], BF16)
    onesrow_f = nc.alloc_sbuf_tensor("onesrow_f", [1, 128], F32)
    a_sb = nc.alloc_sbuf_tensor("a_sb", [RR, D], BF16)
    mask_b = nc.alloc_sbuf_tensor("mask_b", [128, 8 * 128], BF16)

    consts = []
    for v in (EPS, -0.5, -CSHIFT, QSCALE, 1.0 / D, 0.5, -1.0):
        t = nc.alloc_sbuf_tensor(f"cst{len(consts)}", [128, 1], F32)
        nc.const_aps.register(F32, float(v), t[:, :])
        consts.append((t, float(v)))

    ctx = contextlib.ExitStack()

    def sem(name):
        return ctx.enter_context(nc.semaphore(name))

    P = dict(onescol_b=onescol_b, onesrow_b=onesrow_b, onesrow_f=onesrow_f,
             a_sb=a_sb, mask_b=mask_b, consts=consts, dbg_t=dbg_t)

    bar = _Bar(nc, sem)
    P["bar"] = bar
    ccs = [sem("ccs0"), sem("ccs1")]
    with (
        nc.sbuf_tensor("xn", [128, NDT * T], BF16) as xn,
        nc.sbuf_tensor("stat2", [1, 4 * 512], F32) as stat2,
    ):
        w0 = _phase0(nc, sem, P, xn, x0, cfp, mkp, mtf, amask)
        bar.close(w0)
        if "xn1" in DBG:
            dt_ = sem("dbgxn")
            with nc.Block() as _b:
                @_b.gpsimd
                def _(g_):
                    bar.wait(g_)
                    g_.dma_start(dbg_t["xn1"].ap(), xn[:, :]).then_inc(dt_, 16)
                    g_.dma_start(dbg_t["asb"].ap(), P["a_sb"][0:RR, :]
                                 ).then_inc(dt_, 16)
            bar.close([(dt_, 32)])
        w1 = _phase_ffn(nc, sem, "f1", P, xn, wbt, bb, enct, x0, x1,
                        stat_out=stat2)
        bar.close(w1)
        w2 = _phase_ln(nc, sem, "l2", P, xn, x1, stat2)
        bar.close(w2)
        w3 = _phase_qkv(nc, sem, P, xn, wq, wk, wv, qsp, src, dst, ccs)
        bar.close(w3)
    with nc.sbuf_tensor("stat3", [1, 4 * 512], F32) as stat3:
        _phase_attn(nc, sem, P, qsp, dst, wo, x1, x2, ccs, stat_out=stat3)
        with (
            nc.sbuf_tensor("xn2", [128, NDT * T], BF16) as xn2,
        ):
            w4 = _phase_ln(nc, sem, "l3", P, xn2, x2, stat3)
            bar.close(w4)
            w5 = _phase_ffn(nc, sem, "f2", P, xn2, wbt, bb, enct, x2, out,
                            stat_out=None)
            bar.close(w5)
    if DBG:
        _phase_dbg(nc, sem, P, dram, dbg_t)

    ctx.close()
    return nc


def _phase0(nc, sem, P, xn, x0, cfp, mkp, mtf, amask):
    onescol_b, onesrow_f = P["onescol_b"], P["onesrow_f"]
    a_sb, mask_b = P["a_sb"], P["mask_b"]
    NT = 2 * NDT

    with (
        nc.sbuf_tensor("p0_cf", [128, NDT * CF_W], F32) as cf_st,
        nc.sbuf_tensor("p0_mk", [128, NDT * CF_W], U8) as mk_st,
        nc.sbuf_tensor("p0_ws", [128, NDT * RR], BF16) as ws_b,
        nc.sbuf_tensor("p0_wsf", [128, NDT * RR], F32) as wsf,
        nc.sbuf_tensor("p0_mt", [128, 2 * D], BF16) as mt_st,
        nc.sbuf_tensor("p0_xs", [128, 2 * 512], F32) as x_st,
        nc.sbuf_tensor("p0_xb", [128, 2 * 512], BF16) as xb_t,
        nc.sbuf_tensor("p0_sq", [128, 2 * 512], BF16) as sq_t,
        nc.sbuf_tensor("p0_xa", [128, 2 * 512], F32) as xa_st,
        nc.sbuf_tensor("p0_t1", [128, 512], F32) as t1_st,
        nc.sbuf_tensor("p0_rows", [1, 8 * 512], F32) as rows,
        nc.psum_tensor("p0_ps", [128, 8 * 512], F32) as ps,
    ):
        ld = sem("p0_ld")
        mtlp = [sem("p0_mtl0"), sem("p0_mtl1")]
        xlp = [sem("p0_xl0"), sem("p0_xl1")]
        xl2p = [sem("p0_xl2a"), sem("p0_xl2b")]
        wsr = sem("p0_wsr")
        peA = sem("p0_peA")
        aev = sem("p0_aev")
        stm = sem("p0_stm")
        bcr = sem("p0_bcr")
        xbr = sem("p0_xbr")
        sqr = sem("p0_sqr")
        rwa = sem("p0_rwa")
        rwd = sem("p0_rwd")
        dvap = sem("p0_dvap")

        def rowsl(i):
            return rows[:1, i * 512:(i + 1) * 512]

        bar = P["bar"]
        with nc.Block() as blk:
            @blk.sync
            def _(sync):
                bar.wait(sync)
                sync.dma_start(mask_b[:, :], amask.ap()).then_inc(ld, 16)
                sync.dma_start(cf_st[:, :], cfp.ap()).then_inc(ld, 16)
                sync.dma_start(mk_st[:, :], mkp.ap()).then_inc(ld, 16)
                for ot in range(NDT):
                    if ot >= 2:
                        sync.wait_ge(peA, ot - 1)
                    sync.dma_start(
                        mt_st[:, (ot % 2) * D:(ot % 2 + 1) * D],
                        bass.AP(mtf, ot * 128 * D, [[D, 128], [1, D]]),
                    ).then_inc(mtlp[ot % 2], 16)
                for g in range(NT):
                    sc, dt = g // NDT, g % NDT
                    if g >= 2:
                        sync.wait_ge(xbr, g - 1)
                        sync.wait_ge(sqr, g - 1)
                    sync.dma_start(
                        x_st[:, (g % 2) * 512:(g % 2 + 1) * 512],
                        bass.AP(x0, dt * 128 * T + sc * 512,
                                [[T, 128], [1, 512]])).then_inc(xlp[g % 2], 16)
                for g in range(NT):
                    sc, dt = g // NDT, g % NDT
                    if g >= 2:
                        sync.wait_ge(dvap, 2 + g - 1)
                    sync.dma_start(
                        xa_st[:, (g % 2) * 512:(g % 2 + 1) * 512],
                        bass.AP(x0, dt * 128 * T + sc * 512,
                                [[T, 128], [1, 512]])).then_inc(xl2p[g % 2], 16)

            @blk.vector
            def _(v):
                bar.wait(v)
                v.memset(onescol_b[:, :], 1.0)
                v.memset(P["onesrow_b"][:1, :], 1.0)
                v.memset(onesrow_f[:1, :], 1.0)
                for t, val in P["consts"]:
                    v.memset(t[:, :], val)
                v.wait_ge(ld, 16 * 3)
                def prod_ap(e, R, foff):
                    return bass.AP(cf_st, foff + e * R,
                                   [[NDT * CF_W, 128], [CF_W, NDT], [1, R]])

                def wsum_ap(R, roff):
                    return bass.AP(wsf, roff,
                                   [[NDT * RR, 128], [RR, NDT], [1, R]])

                v.tensor_tensor(cf_st[:, :], cf_st[:, :], mk_st[:, :],
                                OP.mult).then_inc(wsr, 1)
                v.wait_ge(wsr, 1)
                v.tensor_tensor(wsum_ap(R0, 0), prod_ap(0, R0, 0),
                                prod_ap(1, R0, 0), OP.add)
                v.tensor_tensor(wsum_ap(R1, R0), prod_ap(0, R1, NE * R0),
                                prod_ap(1, R1, NE * R0), OP.add
                                ).then_inc(wsr, 1)
                v.wait_ge(wsr, 2)
                v.tensor_tensor(wsum_ap(R0, 0), wsum_ap(R0, 0),
                                prod_ap(2, R0, 0), OP.add)
                v.tensor_tensor(wsum_ap(R1, R0), wsum_ap(R1, R0),
                                prod_ap(2, R1, NE * R0), OP.add
                                ).then_inc(wsr, 1)
                v.wait_ge(wsr, 3)
                v.tensor_copy(ws_b[:, :], wsf[:, :]).then_inc(wsr, NDT)
                for sc in range(2):
                    for dt in range(NDT):
                        g = sc * NDT + dt
                        v.wait_ge(xlp[g % 2], 16 * (g // 2 + 1))
                        if g >= 2:
                            v.wait_ge(stm, g - 1)
                        sl = slice((g % 2) * 512, (g % 2 + 1) * 512)
                        v.tensor_copy(xb_t[:, sl], x_st[:, sl]).then_inc(xbr, 1)
                    v.wait_ge(rwa, 2 * sc + 2)
                    v.tensor_tensor(rowsl(4 * sc + 2), rowsl(4 * sc),
                                    rowsl(4 * sc), OP.mult).then_inc(dvap, 1)
                    v.wait_ge(dvap, sc + 1)
                    v.tensor_tensor(rowsl(4 * sc + 2), rowsl(4 * sc + 1),
                                    rowsl(4 * sc + 2), OP.subtract
                                    ).then_inc(rwd, 1)
                for sc in range(2):
                    v.wait_ge(rwa, 5 + sc)
                    v.scalar_tensor_tensor(rowsl(4 * sc + 1), rowsl(4 * sc),
                                           -1.0, rowsl(4 * sc + 2),
                                           OP.mult, OP.mult).then_inc(rwd, 1)
                for sc in range(2):
                    for dt in range(NDT):
                        g = sc * NDT + dt
                        v.wait_ge(bcr, 2 * (sc + 1))
                        v.wait_ge(xl2p[g % 2], 16 * (g // 2 + 1))
                        xa = xa_st[:, (g % 2) * 512:(g % 2 + 1) * 512]
                        v.tensor_tensor(t1_st[:, :], xa,
                                        ps[:, (4 + 2 * sc) * 512:
                                           (5 + 2 * sc) * 512], OP.mult)
                        v.tensor_tensor(
                            xn[:, dt * T + sc * 512:dt * T + sc * 512 + 512],
                            t1_st[:, :],
                            ps[:, (5 + 2 * sc) * 512:(6 + 2 * sc) * 512],
                            OP.add).then_inc(dvap, 1)

            @blk.tensor
            def _(t):
                bar.wait(t)
                for ot in range(NDT):
                    t.wait_ge(mtlp[ot % 2], 16 * (ot // 2 + 1))
                    t.wait_ge(wsr, 3 + NDT)
                    for jc in range(8):
                        mm = t.matmul(
                            ps[0:RR, jc * 512:(jc + 1) * 512],
                            ws_b[:, ot * RR:(ot + 1) * RR],
                            mt_st[:, (ot % 2) * D + jc * 512:
                                  (ot % 2) * D + jc * 512 + 512],
                            start=(ot == 0), stop=(ot == NDT - 1))
                    mm.then_inc(peA, 1)
                t.wait_ge(aev, 8)
                for sc in range(2):
                    if sc == 1:
                        t.wait_ge(rwa, 2)
                    for dt in range(NDT):
                        g = sc * NDT + dt
                        t.wait_ge(xbr, g + 1)
                        t.wait_ge(sqr, g + 1)
                        sl = slice((g % 2) * 512, (g % 2 + 1) * 512)
                        t.matmul(ps[0:1, 0:512], onescol_b[:, :], xb_t[:, sl],
                                 start=(dt == 0), stop=(dt == NDT - 1))
                        t.matmul(ps[0:1, 512:1024], onescol_b[:, :],
                                 sq_t[:, sl], start=(dt == 0),
                                 stop=(dt == NDT - 1)).then_inc(stm, 1)
                for sc in range(2):
                    t.wait_ge(rwd, 2 + sc + 1)
                    t.matmul(ps[:, (4 + 2 * sc) * 512:(5 + 2 * sc) * 512],
                             onesrow_f[:1, :], rowsl(4 * sc + 2),
                             start=True, stop=True).then_inc(bcr, 1)
                    t.matmul(ps[:, (5 + 2 * sc) * 512:(6 + 2 * sc) * 512],
                             onesrow_f[:1, :], rowsl(4 * sc + 1),
                             start=True, stop=True).then_inc(bcr, 1)

            @blk.scalar
            def _(s):
                bar.wait(s)
                s.wait_ge(peA, NDT)
                for jc in range(8):
                    s.activation(a_sb[0:RR, jc * 512:(jc + 1) * 512],
                                 ps[0:RR, jc * 512:(jc + 1) * 512],
                                 AF.Copy).then_inc(aev, 1)
                for sc in range(2):
                    for dt in range(NDT):
                        g = sc * NDT + dt
                        s.wait_ge(xlp[g % 2], 16 * (g // 2 + 1))
                        if g >= 2:
                            s.wait_ge(stm, g - 1)
                        sl = slice((g % 2) * 512, (g % 2 + 1) * 512)
                        s.activation(sq_t[:, sl], x_st[:, sl], AF.Square
                                     ).then_inc(sqr, 1)
                    s.wait_ge(stm, NDT * (sc + 1))
                    s.activation(rowsl(4 * sc), ps[0:1, 0:512], AF.Copy,
                                 scale=1.0 / D).then_inc(rwa, 1)
                    s.activation(rowsl(4 * sc + 1), ps[0:1, 512:1024], AF.Copy,
                                 scale=1.0 / D).then_inc(rwa, 1)
                for sc in range(2):
                    s.wait_ge(rwd, sc + 1)
                    s.activation(rowsl(4 * sc + 3), rowsl(4 * sc + 2),
                                 AF.Ln, bias=EPS)
                    s.activation(rowsl(4 * sc + 2), rowsl(4 * sc + 3),
                                 AF.Exp, scale=-0.5).then_inc(rwa, 1)

    return [(dvap, 2 + 2 * NDT), (aev, 8)]


def _phase_ffn(nc, sem, tag, P, xn, wbt, bb, enct, x_in, x_out, stat_out):
    a_sb = P["a_sb"]
    onesrow_b, onescol_b = P["onesrow_b"], P["onescol_b"]
    NG = 2 * NDT

    with (
        nc.sbuf_tensor(f"{tag}_enc", [128, NDT * RR], F32) as enc_st,
        nc.sbuf_tensor(f"{tag}_encb", [128, NDT * RR], BF16) as enct_b,
        nc.sbuf_tensor(f"{tag}_bst", [1, D], F32) as b_st,
        nc.sbuf_tensor(f"{tag}_b05", [1, D], BF16) as b05_b,
        nc.sbuf_tensor(f"{tag}_esb", [RR, T], BF16) as e_sb,
        nc.sbuf_tensor(f"{tag}_wst", [128, 2 * D], F32) as wst,
        nc.sbuf_tensor(f"{tag}_wbb", [128, 2 * D], BF16) as wbb,
        nc.sbuf_tensor(f"{tag}_xi", [128, 2 * 512], F32) as xi,
        nc.sbuf_tensor(f"{tag}_xo", [128, 2 * 512], F32) as xo,
        nc.sbuf_tensor(f"{tag}_xb", [128, 2 * 512], BF16) as xb,
        nc.sbuf_tensor(f"{tag}_sq", [128, 2 * 512], BF16) as sq,
        nc.psum_tensor(f"{tag}_po", [128, 4 * 512], F32) as po,
        nc.psum_tensor(f"{tag}_pe", [128, 2 * 512], F32) as pev,
        nc.psum_tensor(f"{tag}_pst", [128, 2 * 512], F32) as pst,
    ):
        if "ffn_sems" not in P:
            P["ffn_sems"] = {k: sem("ffn_" + k) for k in (
                "wld0", "wld1", "wcast", "xld0", "xld1", "dva", "xbr", "sqr",
                "stc", "pem", "pee", "eev", "stx0", "stx1", "fin", "eld",
                "ini")}
            P["ffn_tot"] = {k: 0 for k in P["ffn_sems"]}
        SS = P["ffn_sems"]
        B = dict(P["ffn_tot"])
        wldp = [SS["wld0"], SS["wld1"]]
        xldp = [SS["xld0"], SS["xld1"]]
        stxp = [SS["stx0"], SS["stx1"]]
        wcast, dva, xbr, sqr = SS["wcast"], SS["dva"], SS["xbr"], SS["sqr"]
        stc, pem, pee, eev = SS["stc"], SS["pem"], SS["pee"], SS["eev"]
        fin, eld, ini = SS["fin"], SS["eld"], SS["ini"]
        bw = [B["wld0"], B["wld1"]]
        bx = [B["xld0"], B["xld1"]]
        bs = [B["stx0"], B["stx1"]]

        def stat_ps(ssc, which):
            t_ = pev if ssc == 0 else pst
            return t_[0:1, which * 512:(which + 1) * 512]

        bar = P["bar"]
        with nc.Block() as blk:
            @blk.sync
            def _(sync):
                bar.wait(sync)
                sync.dma_start(enc_st[:, :], enct.ap()).then_inc(eld, 16)
                sync.dma_start(b_st[:1, :], bb.ap()).then_inc(eld, 16)
                for ot in range(NDT):
                    if ot >= 2:
                        sync.wait_ge(wcast, B["wcast"] + ot - 1)
                    sync.dma_start(
                        wst[:, (ot % 2) * D:(ot % 2 + 1) * D],
                        bass.AP(wbt, ot * D, [[NDT * D, 128], [1, D]]),
                    ).then_inc(wldp[ot % 2], 16)
                    for g in (2 * ot, 2 * ot + 1):
                        if g >= 2:
                            sync.wait_ge(dva, B["dva"] + g - 1)
                        sync.dma_start(
                            xi[:, (g % 2) * 512:(g % 2 + 1) * 512],
                            bass.AP(x_in, (g // 2) * 128 * T + (g % 2) * 512,
                                    [[T, 128], [1, 512]])
                        ).then_inc(xldp[g % 2], 16)

            @blk.vector
            def _(v):
                bar.wait(v)
                v.wait_ge(eld, B["eld"] + 32)
                v.tensor_copy(enct_b[:, :], enc_st[:, :])
                v.tensor_scalar_mul(b05_b[:1, :], b_st[:1, :], 0.5
                                    ).then_inc(ini, 1)

                def wcast_one(ot):
                    v.wait_ge(wldp[ot % 2], bw[ot % 2] + 16 * (ot // 2 + 1))
                    if ot >= 2:
                        v.wait_ge(pem, B["pem"] + 2 * (ot - 1))
                    v.tensor_scalar_mul(
                        wbb[:, (ot % 2) * D:(ot % 2 + 1) * D],
                        wst[:, (ot % 2) * D:(ot % 2 + 1) * D], 0.5
                    ).then_inc(wcast, 1)

                wcast_one(0)
                wcast_one(1)
                for g in range(NG):
                    v.wait_ge(pem, B["pem"] + g + 1)
                    v.wait_ge(xldp[g % 2], bx[g % 2] + 16 * (g // 2 + 1))
                    if g >= 2:
                        v.wait_ge(stxp[g % 2], bs[g % 2] + 16 * (g // 2))
                        if stat_out is not None:
                            v.wait_ge(sqr, B["sqr"] + g - 1)
                    sl = slice((g % 2) * 512, (g % 2 + 1) * 512)
                    v.tensor_tensor(
                        xo[:, sl], po[:, (g % 4) * 512:(g % 4 + 1) * 512],
                        xi[:, sl], OP.add).then_inc(dva, 1)
                    if stat_out is not None:
                        if g >= 2:
                            v.wait_ge(stc, B["stc"] + g - 1)
                        v.tensor_copy(xb[:, sl], xo[:, sl]).then_inc(xbr, 1)
                    if g % 2 == 1 and g // 2 + 2 < NDT:
                        wcast_one(g // 2 + 2)

            @blk.tensor
            def _(t):
                bar.wait(t)
                t.wait_ge(ini, B["ini"] + 1)
                for sc in range(2):
                    for dt in range(NDT):
                        mm = t.matmul(
                            pev[0:RR, sc * 512:(sc + 1) * 512],
                            enct_b[:, dt * RR:(dt + 1) * RR],
                            xn[:, dt * T + sc * 512:dt * T + sc * 512 + 512],
                            start=(dt == 0), stop=(dt == NDT - 1))
                    mm.then_inc(pee, 1)
                t.wait_ge(eev, B["eev"] + 2)
                for g in range(NG):
                    ot, sc = g // 2, g % 2
                    t.wait_ge(wcast, B["wcast"] + ot + 1)
                    if g >= 4:
                        t.wait_ge(dva, B["dva"] + g - 3)
                    ob = slice((g % 4) * 512, (g % 4 + 1) * 512)
                    t.matmul(po[:, ob], b05_b[0:1, ot * 128:(ot + 1) * 128],
                             onesrow_b[:1, :], start=True, stop=False)
                    for dt in range(NDT):
                        t.matmul(po[:, ob],
                                 wbb[:, (ot % 2) * D + dt * 128:
                                     (ot % 2) * D + (dt + 1) * 128],
                                 xn[:, dt * T + sc * 512:dt * T + sc * 512 + 512],
                                 start=False, stop=False)
                    t.matmul(po[:, ob], a_sb[0:RR, ot * 128:(ot + 1) * 128],
                             e_sb[0:RR, sc * 512:(sc + 1) * 512],
                             start=False, stop=True).then_inc(pem, 1)
                    if stat_out is not None and g >= 2:
                        gs = g - 2
                        t.wait_ge(xbr, B["xbr"] + gs + 1)
                        t.wait_ge(sqr, B["sqr"] + gs + 1)
                        ssc = gs % 2
                        ww = gs // 2
                        sl = slice((gs % 2) * 512, (gs % 2 + 1) * 512)
                        t.matmul(stat_ps(ssc, 0), onescol_b[:, :], xb[:, sl],
                                 start=(ww == 0), stop=(ww == NDT - 1))
                        t.matmul(stat_ps(ssc, 1), onescol_b[:, :], sq[:, sl],
                                 start=(ww == 0), stop=(ww == NDT - 1)
                                 ).then_inc(stc, 1)
                if stat_out is not None:
                    for gs in range(NG - 2, NG):
                        t.wait_ge(xbr, B["xbr"] + gs + 1)
                        t.wait_ge(sqr, B["sqr"] + gs + 1)
                        ssc = gs % 2
                        ww = gs // 2
                        sl = slice((gs % 2) * 512, (gs % 2 + 1) * 512)
                        t.matmul(stat_ps(ssc, 0), onescol_b[:, :], xb[:, sl],
                                 start=(ww == 0), stop=(ww == NDT - 1))
                        t.matmul(stat_ps(ssc, 1), onescol_b[:, :], sq[:, sl],
                                 start=(ww == 0), stop=(ww == NDT - 1)
                                 ).then_inc(stc, 1)

            @blk.scalar
            def _(s):
                bar.wait(s)
                for sc in range(2):
                    s.wait_ge(pee, B["pee"] + sc + 1)
                    s.activation(e_sb[0:RR, sc * 512:(sc + 1) * 512],
                                 pev[0:RR, sc * 512:(sc + 1) * 512],
                                 AF.Copy).then_inc(eev, 1)
                if stat_out is not None:
                    for g in range(NG):
                        s.wait_ge(dva, B["dva"] + g + 1)
                        if g >= 2:
                            s.wait_ge(stc, B["stc"] + g - 1)
                        sl = slice((g % 2) * 512, (g % 2 + 1) * 512)
                        s.activation(sq[:, sl], xo[:, sl], AF.Square
                                     ).then_inc(sqr, 1)
                    s.wait_ge(stc, B["stc"] + NG)
                    for ssc in range(2):
                        s.activation(
                            stat_out[:1, 2 * ssc * 512:(2 * ssc + 1) * 512],
                            stat_ps(ssc, 0), AF.Copy, scale=1.0 / D
                        ).then_inc(fin, 1)
                        s.activation(
                            stat_out[:1, (2 * ssc + 1) * 512:
                                     (2 * ssc + 2) * 512],
                            stat_ps(ssc, 1), AF.Copy, scale=1.0 / D
                        ).then_inc(fin, 1)

            @blk.gpsimd
            def _(g_):
                bar.wait(g_)
                for g in range(NG):
                    ot, sc = g // 2, g % 2
                    g_.wait_ge(dva, B["dva"] + g + 1)
                    g_.dma_start(
                        bass.AP(x_out, ot * 128 * T + sc * 512,
                                [[T, 128], [1, 512]]),
                        xo[:, (g % 2) * 512:(g % 2 + 1) * 512],
                    ).then_inc(stxp[g % 2], 16)
                g_.wait_ge(stxp[0], bs[0] + 16 * (NG // 2))
                g_.wait_ge(stxp[1], bs[1] + 16 * (NG // 2))
                if stat_out is not None:
                    g_.wait_ge(fin, B["fin"] + 4)
                if tag == "f1" and "esb" in P.get("dbg_t", {}):
                    dt_ = sem("dbges")
                    g_.dma_start(P["dbg_t"]["esb"].ap(), e_sb[0:RR, :]
                                 ).then_inc(dt_, 16)
                    g_.wait_ge(dt_, 16)

    tot = P["ffn_tot"]
    for k, d in (("wld0", 256), ("wld1", 256), ("xld0", 512), ("xld1", 512),
                 ("wcast", 32), ("dva", 64), ("pem", 64), ("pee", 2),
                 ("eev", 2), ("eld", 32), ("ini", 1), ("stx0", 512),
                 ("stx1", 512)):
        tot[k] += d
    if stat_out is not None:
        for k, d in (("xbr", 64), ("sqr", 64), ("stc", 64), ("fin", 4)):
            tot[k] += d
    waits = [(stxp[0], tot["stx0"]), (stxp[1], tot["stx1"])]
    if stat_out is not None:
        waits.append((fin, tot["fin"]))
    return waits


def _phase_ln(nc, sem, tag, P, xn, x_src, stat):
    onesrow_f = P["onesrow_f"]
    NT = 2 * NDT

    with (
        nc.sbuf_tensor(f"{tag}_xa", [128, 2 * 512], F32) as xa,
        nc.sbuf_tensor(f"{tag}_t1", [128, 512], F32) as t1,
        nc.sbuf_tensor(f"{tag}_rows", [1, 4 * 512], F32) as rows,
        nc.psum_tensor(f"{tag}_bc", [128, 4 * 512], F32) as bc,
    ):
        if "ln_sems" not in P:
            P["ln_sems"] = {k: sem("ln_" + k) for k in (
                "xl0", "xl1", "rwa", "rwd", "bcr", "ap")}
            P["ln_tot"] = {k: 0 for k in P["ln_sems"]}
        SS = P["ln_sems"]
        B = dict(P["ln_tot"])
        xlp = [SS["xl0"], SS["xl1"]]
        rwa, rwd, bcr, ap = SS["rwa"], SS["rwd"], SS["bcr"], SS["ap"]
        bx = [B["xl0"], B["xl1"]]

        def rowsl(i):
            return rows[:1, i * 512:(i + 1) * 512]

        bar = P["bar"]
        with nc.Block() as blk:
            @blk.sync
            def _(sync):
                bar.wait(sync)
                for g in range(NT):
                    sc, dt = g // NDT, g % NDT
                    if g >= 2:
                        sync.wait_ge(ap, B["ap"] + 2 + g - 1)
                    sync.dma_start(
                        xa[:, (g % 2) * 512:(g % 2 + 1) * 512],
                        bass.AP(x_src, dt * 128 * T + sc * 512,
                                [[T, 128], [1, 512]])).then_inc(xlp[g % 2], 16)

            @blk.vector
            def _(v):
                bar.wait(v)
                for sc in range(2):
                    m_ = stat[:1, 2 * sc * 512:(2 * sc + 1) * 512]
                    e_ = stat[:1, (2 * sc + 1) * 512:(2 * sc + 2) * 512]
                    va = rowsl(2 * sc)
                    v.tensor_tensor(va, m_, m_, OP.mult).then_inc(ap, 1)
                    v.wait_ge(ap, B["ap"] + sc + 1)
                    v.tensor_tensor(va, e_, va, OP.subtract).then_inc(rwd, 1)
                for sc in range(2):
                    v.wait_ge(rwa, B["rwa"] + sc + 1)
                    m_ = stat[:1, 2 * sc * 512:(2 * sc + 1) * 512]
                    v.scalar_tensor_tensor(rowsl(2 * sc + 1), m_, -1.0,
                                           rowsl(2 * sc), OP.mult, OP.mult
                                           ).then_inc(rwd, 1)
                for sc in range(2):
                    for dt in range(NDT):
                        g = sc * NDT + dt
                        v.wait_ge(bcr, B["bcr"] + 2 * (sc + 1))
                        v.wait_ge(xlp[g % 2], bx[g % 2] + 16 * (g // 2 + 1))
                        sl = slice((g % 2) * 512, (g % 2 + 1) * 512)
                        v.tensor_tensor(t1[:, :], xa[:, sl],
                                        bc[:, 2 * sc * 512:(2 * sc + 1) * 512],
                                        OP.mult)
                        v.tensor_tensor(
                            xn[:, dt * T + sc * 512:dt * T + sc * 512 + 512],
                            t1[:, :],
                            bc[:, (2 * sc + 1) * 512:(2 * sc + 2) * 512],
                            OP.add).then_inc(ap, 1)

            @blk.scalar
            def _(s):
                bar.wait(s)
                for sc in range(2):
                    s.wait_ge(rwd, B["rwd"] + sc + 1)
                    tmp = rowsl(3)
                    s.activation(tmp, rowsl(2 * sc), AF.Ln, bias=EPS)
                    s.activation(rowsl(2 * sc), tmp, AF.Exp, scale=-0.5
                                 ).then_inc(rwa, 1)

            @blk.tensor
            def _(t):
                bar.wait(t)
                for sc in range(2):
                    t.wait_ge(rwd, B["rwd"] + 2 + sc + 1)
                    t.matmul(bc[:, 2 * sc * 512:(2 * sc + 1) * 512],
                             onesrow_f[:1, :], rowsl(2 * sc),
                             start=True, stop=True).then_inc(bcr, 1)
                    t.matmul(bc[:, (2 * sc + 1) * 512:(2 * sc + 2) * 512],
                             onesrow_f[:1, :], rowsl(2 * sc + 1),
                             start=True, stop=True).then_inc(bcr, 1)

    tot = P["ln_tot"]
    for k, d in (("xl0", 16 * 32), ("xl1", 16 * 32), ("rwa", 2), ("rwd", 4),
                 ("bcr", 4), ("ap", 2 + NT)):
        tot[k] += d
    return [(ap, tot["ap"])]


def _phase_qkv(nc, sem, P, xn, wq, wk, wv, qsp, src, dst, ccs):
    NV = 2 * 8 * NDT

    with (
        nc.sbuf_tensor("qk_wst", [128, 2 * D], F32) as wst,
        nc.sbuf_tensor("qk_wbb", [128, 2 * D], BF16) as wbb,
        nc.sbuf_tensor("qk_vw", [128, 4 * 512], F32) as vw,
        nc.sbuf_tensor("qk_vwb", [128, 4 * 512], BF16) as vwb,
        nc.sbuf_tensor("qk_st", [128, 4 * 512], BF16) as stg,
        nc.psum_tensor("qk_po", [128, 8 * 512], F32) as po,
    ):
        wldp = [sem("qk_wld0"), sem("qk_wld1")]
        wcast = sem("qk_wcast")
        vldp = [sem(f"qk_vld{j}") for j in range(4)]
        vcast = sem("qk_vcast")
        pei = sem("qk_pei")
        pem = sem("qk_pem")
        aev = sem("qk_aev")
        avv = sem("qk_avv")
        stkp = [sem(f"qk_stk{j}") for j in range(4)]
        stqp = [sem(f"qk_stq{j}") for j in range(4)]

        bar = P["bar"]
        with nc.Block() as blk:
            @blk.sync
            def _(sync):
                bar.wait(sync)
                for et in range(NDT):
                    if et >= 2:
                        sync.wait_ge(wcast, et - 1)
                    sync.dma_start(
                        wst[:, (et % 2) * D:(et % 2 + 1) * D],
                        bass.AP(wk, et * D, [[NDT * D, 128], [1, D]]),
                    ).then_inc(wldp[et % 2], 16)
                for i in range(NV):
                    dt = i % NDT
                    ev = (i // NDT) % 8
                    if i >= 4:
                        sync.wait_ge(vcast, i - 3)
                    sync.dma_start(
                        vw[:, (i % 4) * 512:(i % 4 + 1) * 512],
                        bass.AP(wv, dt * 128 * D + ev * 512,
                                [[D, 128], [1, 512]])
                    ).then_inc(vldp[i % 4], 16)
                for et in range(NDT):
                    if et >= 2:
                        sync.wait_ge(wcast, NDT + et - 1)
                    sync.dma_start(
                        wst[:, (et % 2) * D:(et % 2 + 1) * D],
                        bass.AP(wq, et * D, [[NDT * D, 128], [1, D]]),
                    ).then_inc(wldp[et % 2], 16)

            @blk.vector
            def _(v):
                bar.wait(v)
                for et in range(NDT):
                    v.wait_ge(wldp[et % 2], 16 * (et // 2 + 1))
                    if et >= 2:
                        v.wait_ge(pem, 2 * (et - 1))
                    v.tensor_copy(wbb[:, (et % 2) * D:(et % 2 + 1) * D],
                                  wst[:, (et % 2) * D:(et % 2 + 1) * D]
                                  ).then_inc(wcast, 1)
                for i in range(NV):
                    v.wait_ge(vldp[i % 4], 16 * (i // 4 + 1))
                    if i >= 4:
                        v.wait_ge(pei, i - 3)
                    v.tensor_copy(vwb[:, (i % 4) * 512:(i % 4 + 1) * 512],
                                  vw[:, (i % 4) * 512:(i % 4 + 1) * 512]
                                  ).then_inc(vcast, 1)
                for et in range(NDT):
                    v.wait_ge(wldp[et % 2], 16 * ((NDT + et) // 2 + 1))
                    if et >= 2:
                        v.wait_ge(pem, 64 + 2 * (et - 1))
                    v.tensor_copy(wbb[:, (et % 2) * D:(et % 2 + 1) * D],
                                  wst[:, (et % 2) * D:(et % 2 + 1) * D]
                                  ).then_inc(wcast, 1)

            @blk.tensor
            def _(t):
                bar.wait(t)
                for et in range(NDT):
                    t.wait_ge(wcast, et + 1)
                    for sc in range(2):
                        g = et * 2 + sc
                        if g >= 4:
                            t.wait_ge(aev, g - 3)
                        ob = slice((g % 4) * 512, (g % 4 + 1) * 512)
                        for dt in range(NDT):
                            mm = t.matmul(
                                po[:, ob],
                                wbb[:, (et % 2) * D + dt * 128:
                                    (et % 2) * D + (dt + 1) * 128],
                                xn[:, dt * T + sc * 512:dt * T + sc * 512 + 512],
                                start=(dt == 0), stop=(dt == NDT - 1))
                        mm.then_inc(pem, 1)
                t.wait_ge(aev, 64)
                for half in range(2):
                    for ev in range(8):
                        ge = half * 8 + ev
                        if ge >= 1:
                            t.wait_ge(avv, 4 * ge)
                        for dt in range(NDT):
                            i = half * 8 * NDT + ev * NDT + dt
                            t.wait_ge(vcast, i + 1)
                            for st4 in range(4):
                                stt = half * 4 + st4
                                mm = t.matmul(
                                    po[:, st4 * 512:(st4 + 1) * 512],
                                    xn[:, dt * T + stt * 128:
                                       dt * T + (stt + 1) * 128],
                                    vwb[:, (i % 4) * 512:(i % 4 + 1) * 512],
                                    start=(dt == 0), stop=(dt == NDT - 1))
                            mm.then_inc(pei, 1)
                            del mm
                t.wait_ge(avv, 64)
                for et in range(NDT):
                    t.wait_ge(wcast, NDT + et + 1)
                    for sc in range(2):
                        g = 64 + et * 2 + sc
                        if g >= 68:
                            t.wait_ge(aev, g - 3)
                        ob = slice((g % 4) * 512, (g % 4 + 1) * 512)
                        for dt in range(NDT):
                            mm = t.matmul(
                                po[:, ob],
                                wbb[:, (et % 2) * D + dt * 128:
                                    (et % 2) * D + (dt + 1) * 128],
                                xn[:, dt * T + sc * 512:dt * T + sc * 512 + 512],
                                start=(dt == 0), stop=(dt == NDT - 1))
                        mm.then_inc(pem, 1)

            @blk.scalar
            def _(s):
                bar.wait(s)
                for g in range(64):
                    s.wait_ge(pem, g + 1)
                    if g >= 4:
                        s.wait_ge(stkp[g % 4], 16 * (g // 4))
                    s.activation(stg[:, (g % 4) * 512:(g % 4 + 1) * 512],
                                 po[:, (g % 4) * 512:(g % 4 + 1) * 512],
                                 AF.Copy).then_inc(aev, 1)
                for ge in range(16):
                    s.wait_ge(pei, (ge + 1) * NDT)
                    for st4 in range(4):
                        j = 4 * ge + st4
                        if j >= 4:
                            s.wait_ge(stkp[j % 4], 16 * (16 + j // 4))
                        else:
                            s.wait_ge(stkp[j % 4], 16 * 16)
                        s.activation(stg[:, st4 * 512:(st4 + 1) * 512],
                                     po[:, st4 * 512:(st4 + 1) * 512],
                                     AF.Copy).then_inc(avv, 1)
                for g in range(64):
                    s.wait_ge(pem, 64 + g + 1)
                    if g < 4:
                        s.wait_ge(stkp[g % 4], 16 * 32)
                    else:
                        s.wait_ge(stqp[g % 4], 16 * (g // 4))
                    s.activation(stg[:, (g % 4) * 512:(g % 4 + 1) * 512],
                                 po[:, (g % 4) * 512:(g % 4 + 1) * 512],
                                 AF.Copy, scale=QSCALE).then_inc(aev, 1)

            @blk.gpsimd
            def _(g_):
                bar.wait(g_)
                for g in range(64):
                    et, sc = g // 2, g % 2
                    g_.wait_ge(aev, g + 1)
                    g_.dma_start(
                        bass.AP(src[sc], et * 128 * 512,
                                [[512, 128], [1, 512]]),
                        stg[:, (g % 4) * 512:(g % 4 + 1) * 512],
                    ).then_inc(stkp[g % 4], 16)
                for ge in range(16):
                    half, ev = ge // 8, ge % 8
                    for st4 in range(4):
                        j = 4 * ge + st4
                        g_.wait_ge(avv, j + 1)
                        g_.dma_start(
                            bass.AP(src[half],
                                    D * 512 + st4 * 128 * D + ev * 512,
                                    [[D, 128], [1, 512]]),
                            stg[:, st4 * 512:(st4 + 1) * 512],
                        ).then_inc(stkp[j % 4], 16)
                    if ge == 7:
                        for s4 in range(4):
                            g_.wait_ge(stkp[s4], 16 * 24)
                        g_.collective_compute(
                            "AllGather", OP.bypass,
                            replica_groups=[list(range(NCORES))],
                            ins=[src[0].ap().opt()], outs=[dst[0].ap().opt()],
                        ).then_inc(ccs[0], 1)
                    if ge == 15:
                        for s4 in range(4):
                            g_.wait_ge(stkp[s4], 16 * 32)
                        g_.collective_compute(
                            "AllGather", OP.bypass,
                            replica_groups=[list(range(NCORES))],
                            ins=[src[1].ap().opt()], outs=[dst[1].ap().opt()],
                        ).then_inc(ccs[1], 1)
                for g in range(64):
                    et, sc = g // 2, g % 2
                    g_.wait_ge(aev, 64 + g + 1)
                    g_.dma_start(
                        bass.AP(qsp, et * 128 * T + sc * 512,
                                [[T, 128], [1, 512]]),
                        stg[:, (g % 4) * 512:(g % 4 + 1) * 512],
                    ).then_inc(stqp[g % 4], 16)
                for s4 in range(4):
                    g_.wait_ge(stqp[s4], 16 * 16)

    return ([(s_, 16 * 16) for s_ in stqp] + [(s_, 16 * 32) for s_ in stkp])


def _phase_attn(nc, sem, P, qsp, dst, wo, x1, x2, ccs, stat_out):
    bar = P["bar"]
    mask_b, onesrow_f, onescol_b = P["mask_b"], P["onesrow_f"], P["onescol_b"]
    NW = NDT
    RKB = 2 * D * 512

    qldp = [sem("at_qld0"), sem("at_qld1")]
    kvg = [sem("at_kvg0"), sem("at_kvg1")]
    ptsr = sem("at_ptsr")
    elld = sem("at_elld")
    scd = sem("at_scd")
    expd = sem("at_expd")
    elev = sem("at_elev")
    pvd = sem("at_pvd")
    pvev = sem("at_pvev")
    bcm = sem("at_bcm")
    nrm = sem("at_nrm")
    rcp = sem("at_rcp")
    wldp = [sem("at_wld0"), sem("at_wld1")]
    wcast = sem("at_wcast")
    pem = sem("at_pem")
    xldp = [sem("at_xld0"), sem("at_xld1")]
    dva = sem("at_dva")
    xbr = sem("at_xbr")
    sqr = sem("at_sqr")
    stc = sem("at_stc")
    fin = sem("at_fin")
    stxp = [sem("at_stx0"), sem("at_stx1")]
    msz = sem("at_msz")

    GRP_ALL = GRP3[0] + GRP3[1]

    for half in (0, 1):
        hsteps = ATT3[half]
        hgroups = GRP3[half]

        with nc.sbuf_tensor(f"at{half}_af", [128, H * 512], BF16) as af:
            with (
                nc.sbuf_tensor(f"at{half}_oa", [128, H * 512], BF16) as oa,
                nc.sbuf_tensor(f"at{half}_ell", [1, H * 512], BF16) as ell,
            ):
                with (
                    nc.sbuf_tensor(f"at{half}_q", [128, 2 * 1024], BF16
                                   ) as q_sb,
                    nc.sbuf_tensor(f"at{half}_k", [128, 2 * 8192], BF16) as kr,
                    nc.sbuf_tensor(f"at{half}_v", [128, 2 * 8192], BF16) as vr,
                    nc.sbuf_tensor(f"at{half}_p", [128, 2 * 2048], BF16) as pt,
                    nc.sbuf_tensor(f"at{half}_ps", [128, 2 * 512], BF16
                                   ) as ptsum,
                    nc.psum_tensor(f"at{half}_sc", [128, 6 * 512], F32) as psc,
                    nc.psum_tensor(f"at{half}_pv", [128, 512], F32) as ppv,
                    nc.psum_tensor(f"at{half}_el", [128, 512], F32) as pel,
                ):
                    with nc.Block() as blk:
                        @blk.sync
                        def _(sync, half=half):
                            bar.wait(sync)
                            for hp in range(16):
                                pg = half * 16 + hp
                                if pg >= 2:
                                    sync.wait_ge(
                                        scd, 4 * (PAIR_LAST_QK[pg - 2] + 1))
                                for h2 in (0, 1):
                                    sync.dma_start(
                                        q_sb[:, (pg % 2) * 1024 + h2 * 512:
                                             (pg % 2) * 1024 + h2 * 512 + 512],
                                        bass.AP(qsp,
                                                (2 * hp + h2) * 128 * T +
                                                half * 512,
                                                [[T, 128], [1, 512]]),
                                    ).then_inc(qldp[pg % 2], 16)

                        @blk.gpsimd
                        def _(g_, half=half, hgroups=hgroups):
                            bar.wait(g_)
                            g_.wait_ge(ccs[0], 1)
                            if half == 1:
                                g_.wait_ge(ccs[1], 1)
                            for grp in hgroups:
                                g, hp, hsrc = grp["g"], grp["hp"], grp["hsrc"]
                                if g >= 2:
                                    g_.wait_ge(
                                        pvd, GRP_ALL[g - 2]["last_use"] + 1)
                                sb = (g % 2) * 8192
                                for h2 in (0, 1):
                                    g_.dma_start(
                                        kr[:, sb + h2 * 4096:
                                           sb + h2 * 4096 + 4096],
                                        bass.AP(dst[hsrc],
                                                (2 * hp + h2) * 128 * 512,
                                                [[512, 128], [RKB, NCORES],
                                                 [1, 512]]),
                                    ).then_inc(kvg[g % 2], 16)
                                for s in range(4):
                                    g_.dma_start(
                                        bass.AP(vr, sb + s * 256,
                                                [[2 * 8192, 128],
                                                 [1024, 8], [1, 256]]),
                                        bass.AP(dst[hsrc],
                                                D * 512 + (hp // 2) * 512 +
                                                ((2 * hp) % 4) * 128 +
                                                s * 128 * D,
                                                [[D, 128], [RKB, 8],
                                                 [1, 256]]),
                                    ).then_inc(kvg[g % 2], 16)

                        @blk.tensor
                        def _(t, half=half, hsteps=hsteps):
                            bar.wait(t)
                            t.wait_ge(msz, half + 1)

                            def ell_pv(st):
                                n, ha, cc2 = st["n"], st["ha"], st["cc2"]
                                vb = ((st["g"] % 2) * 8192 + cc2 * 1024 +
                                      st["h2"] * 128)
                                if st["hfirst"] and ha >= 1:
                                    t.wait_ge(elev, ha)
                                t.wait_ge(ptsr, n + 1)
                                t.matmul(
                                    pel[0:1, 0:512],
                                    onescol_b[:, :],
                                    ptsum[:, (n % 2) * 512:(n % 2 + 1) * 512],
                                    start=st["hfirst"], stop=st["hlast"]
                                ).then_inc(elld, 1)
                                if st["hfirst"] and ha >= 1:
                                    t.wait_ge(pvev, ha)
                                for s in range(4):
                                    f0 = 128 * s if st["diag"] else 0
                                    mm2 = t.matmul(
                                        ppv[:, f0:512],
                                        vr[:, vb + s * 256:
                                           vb + s * 256 + 128],
                                        pt[:, (n % 2) * 2048 + s * 512 + f0:
                                           (n % 2) * 2048 + (s + 1) * 512],
                                        start=(st["hfirst"] and s == 0),
                                        stop=(st["hlast"] and s == 3))
                                mm2.then_inc(pvd, 1)
                                del mm2

                            for i, st in enumerate(hsteps):
                                n, cc2, h2 = st["n"], st["cc2"], st["h2"]
                                kb = ((st["g"] % 2) * 8192 + h2 * 4096 +
                                      cc2 * 512)
                                if st["gfirst"]:
                                    t.wait_ge(kvg[st["g"] % 2],
                                              96 * (st["g"] // 2 + 1))
                                if st["pfirst"]:
                                    t.wait_ge(qldp[st["pg"] % 2],
                                              32 * (st["pg"] // 2 + 1))
                                qb = (st["pg"] % 2) * 1024 + h2 * 512
                                for s in range(4):
                                    k = 4 * n + s
                                    if k >= 6:
                                        t.wait_ge(expd, k - 5)
                                    f0 = 128 * s if st["diag"] else 0
                                    kbank = (k % 6) * 512
                                    t.matmul(
                                        psc[:, kbank + f0:kbank + 512],
                                        kr[:, kb + s * 128:kb + s * 128 + 128],
                                        q_sb[:, qb + f0:qb + 512],
                                        start=True, stop=True
                                    ).then_inc(scd, 1)
                                if i >= 1:
                                    ell_pv(hsteps[i - 1])
                            ell_pv(hsteps[-1])

                        @blk.scalar
                        def _(s_, half=half, hsteps=hsteps):
                            bar.wait(s_)
                            for st in hsteps:
                                n = st["n"]
                                if n >= 2:
                                    s_.wait_ge(pvd, n - 1)
                                for s in range(4):
                                    k = 4 * n + s
                                    s_.wait_ge(scd, k + 1)
                                    f0 = 128 * s if st["diag"] else 0
                                    kbank = (k % 6) * 512
                                    s_.activation(
                                        pt[:, (n % 2) * 2048 + s * 512 + f0:
                                           (n % 2) * 2048 + (s + 1) * 512],
                                        psc[:, kbank + f0:kbank + 512],
                                        AF.Exp, bias=-CSHIFT
                                    ).then_inc(expd, 1)

                        @blk.vector
                        def _(v, half=half, hsteps=hsteps):
                            bar.wait(v)
                            v.memset(psc[:, :], 0.0)
                            v.memset(pel[:, :], 0.0)
                            v.nop().then_inc(msz, 1)
                            for st in hsteps:
                                n, cc2, h = st["n"], st["cc2"], st["h"]
                                ha = st["ha"]
                                base = (n % 2) * 2048
                                b2 = (n % 2) * 512
                                v.wait_ge(expd, 4 * (n + 1))
                                if st["diag"]:
                                    for s in range(4):
                                        f0 = 128 * s
                                        sl_ = pt[:, base + s * 512 + f0:
                                                 base + s * 512 + f0 + 128]
                                        ins = v.tensor_tensor(
                                            sl_, sl_,
                                            mask_b[:, cc2 * 128:
                                                   (cc2 + 1) * 128],
                                            OP.mult)
                                    del ins
                                if n >= 2:
                                    v.wait_ge(elld, n - 1)
                                if st["diag"]:
                                    v.tensor_copy(ptsum[:, b2:b2 + 512],
                                                  pt[:, base:base + 512])
                                    for s in range(1, 4):
                                        f0 = 128 * s
                                        ins = v.tensor_tensor(
                                            ptsum[:, b2 + f0:b2 + 512],
                                            ptsum[:, b2 + f0:b2 + 512],
                                            pt[:, base + s * 512 + f0:
                                               base + (s + 1) * 512],
                                            OP.add)
                                    ins.then_inc(ptsr, 1)
                                    del ins
                                else:
                                    ssl = ptsum[:, b2:b2 + 512]
                                    v.tensor_tensor(ssl,
                                                    pt[:, base:base + 512],
                                                    pt[:, base + 512:
                                                       base + 1024], OP.add)
                                    v.tensor_tensor(ssl, ssl,
                                                    pt[:, base + 1024:
                                                       base + 1536], OP.add)
                                    v.tensor_tensor(ssl, ssl,
                                                    pt[:, base + 1536:
                                                       base + 2048], OP.add
                                                    ).then_inc(ptsr, 1)
                                if st["hlast"]:
                                    v.wait_ge(pvd, n + 1)
                                    v.tensor_copy(
                                        oa[:, h * 512:(h + 1) * 512],
                                        ppv[:, 0:512]).then_inc(pvev, 1)
                                    v.tensor_copy(
                                        ell[0:1, h * 512:(h + 1) * 512],
                                        pel[0:1, 0:512]
                                    ).then_inc(elev, 1)

                bar.close([(pvev, H * (half + 1)), (elev, H * (half + 1))])

                with (
                    nc.sbuf_tensor(f"at{half}_inv", [1, 4 * 512], F32) as inv,
                    nc.psum_tensor(f"at{half}_bc", [128, 4 * 512], F32) as pbc,
                ):
                    with nc.Block() as blk:
                        @blk.vector
                        def _(v, half=half):
                            bar.wait(v)
                            for hh in range(4):
                                v.reciprocal(
                                    inv[:1, hh * 512:(hh + 1) * 512],
                                    ell[0:1, hh * 512:(hh + 1) * 512]
                                ).then_inc(rcp, 1)
                            for hh in range(H):
                                idx = half * H + hh
                                v.wait_ge(bcm, idx + 1)
                                if hh + 4 < H:
                                    h4 = hh + 4
                                    v.reciprocal(
                                        inv[:1, (h4 % 4) * 512:
                                            (h4 % 4 + 1) * 512],
                                        ell[0:1, h4 * 512:(h4 + 1) * 512]
                                    ).then_inc(rcp, 1)
                                v.tensor_tensor(
                                    af[:, hh * 512:(hh + 1) * 512],
                                    oa[:, hh * 512:(hh + 1) * 512],
                                    pbc[:, (hh % 4) * 512:(hh % 4 + 1) * 512],
                                    OP.mult).then_inc(nrm, 1)

                        @blk.tensor
                        def _(t, half=half):
                            bar.wait(t)
                            for hh in range(H):
                                idx = half * H + hh
                                t.wait_ge(rcp, idx + 1)
                                if hh >= 4:
                                    t.wait_ge(nrm, idx - 3)
                                t.matmul(
                                    pbc[:, (hh % 4) * 512:(hh % 4 + 1) * 512],
                                    onesrow_f[:1, :],
                                    inv[:1, (hh % 4) * 512:(hh % 4 + 1) * 512],
                                    start=True, stop=True).then_inc(bcm, 1)

            bar.close([(nrm, H * (half + 1))])

            with (
                nc.sbuf_tensor(f"at{half}_wst", [128, 2 * D], F32) as wst,
                nc.sbuf_tensor(f"at{half}_wbb", [128, 2 * D], BF16) as wbb,
                nc.sbuf_tensor(f"at{half}_xi", [128, 2 * 512], F32) as xi,
                nc.sbuf_tensor(f"at{half}_xo", [128, 2 * 512], F32) as xo,
                nc.sbuf_tensor(f"at{half}_xb", [128, 2 * 512], BF16) as xb,
                nc.sbuf_tensor(f"at{half}_sq", [128, 2 * 512], BF16) as sq,
                nc.psum_tensor(f"at{half}_wo", [128, 2 * 512], F32) as pwo,
                nc.psum_tensor(f"at{half}_st", [128, 2 * 512], F32) as pstt,
            ):
                with nc.Block() as blk:
                    @blk.sync
                    def _(sync, half=half):
                        bar.wait(sync)
                        for ot in range(NDT):
                            w = half * NW + ot
                            if ot >= 2:
                                sync.wait_ge(wcast, w - 1)
                            sync.dma_start(
                                wst[:, (ot % 2) * D:(ot % 2 + 1) * D],
                                bass.AP(wo, ot * D,
                                        [[NDT * D, 128], [1, D]]),
                            ).then_inc(wldp[w % 2], 16)
                            if ot >= 2:
                                sync.wait_ge(dva, w - 1)
                            sync.dma_start(
                                xi[:, (ot % 2) * 512:(ot % 2 + 1) * 512],
                                bass.AP(x1, ot * 128 * T + half * 512,
                                        [[T, 128], [1, 512]]),
                            ).then_inc(xldp[w % 2], 16)

                    @blk.vector
                    def _(v, half=half):
                        bar.wait(v)

                        def wcast_one(ot):
                            w = half * NW + ot
                            v.wait_ge(wldp[w % 2], 16 * (w // 2 + 1))
                            if ot >= 2:
                                v.wait_ge(pem, w - 1)
                            v.tensor_copy(
                                wbb[:, (ot % 2) * D:(ot % 2 + 1) * D],
                                wst[:, (ot % 2) * D:(ot % 2 + 1) * D]
                            ).then_inc(wcast, 1)

                        wcast_one(0)
                        wcast_one(1)
                        for ot in range(NDT):
                            w = half * NW + ot
                            v.wait_ge(pem, w + 1)
                            v.wait_ge(xldp[w % 2], 16 * (w // 2 + 1))
                            if ot >= 2:
                                v.wait_ge(stxp[w % 2], 16 * (w // 2))
                                if stat_out is not None:
                                    v.wait_ge(sqr, w - 1)
                            sl = slice((ot % 2) * 512, (ot % 2 + 1) * 512)
                            v.tensor_tensor(
                                xo[:, sl],
                                pwo[:, (ot % 2) * 512:(ot % 2 + 1) * 512],
                                xi[:, sl], OP.add).then_inc(dva, 1)
                            if stat_out is not None:
                                if w >= 2:
                                    v.wait_ge(stc, w - 1)
                                v.tensor_copy(xb[:, sl], xo[:, sl]
                                              ).then_inc(xbr, 1)
                            if ot + 2 < NDT:
                                wcast_one(ot + 2)

                    @blk.tensor
                    def _(t, half=half):
                        bar.wait(t)
                        for ot in range(NDT):
                            w = half * NW + ot
                            t.wait_ge(wcast, w + 1)
                            if ot >= 2:
                                t.wait_ge(dva, w - 1)
                            ob = slice((ot % 2) * 512, (ot % 2 + 1) * 512)
                            for hh in range(H):
                                mm = t.matmul(
                                    pwo[:, ob],
                                    wbb[:, (ot % 2) * D + hh * 128:
                                        (ot % 2) * D + (hh + 1) * 128],
                                    af[:, hh * 512:(hh + 1) * 512],
                                    start=(hh == 0), stop=(hh == H - 1))
                            mm.then_inc(pem, 1)
                            del mm
                            if stat_out is not None and ot >= 2:
                                ws = ot - 2
                                w2 = half * NW + ws
                                t.wait_ge(xbr, w2 + 1)
                                t.wait_ge(sqr, w2 + 1)
                                sl = slice((ws % 2) * 512, (ws % 2 + 1) * 512)
                                t.matmul(pstt[0:1, 0:512], onescol_b[:, :],
                                         xb[:, sl], start=(ws == 0),
                                         stop=(ws == NDT - 1))
                                t.matmul(pstt[0:1, 512:1024], onescol_b[:, :],
                                         sq[:, sl], start=(ws == 0),
                                         stop=(ws == NDT - 1)).then_inc(stc, 1)
                        if stat_out is not None:
                            for ws in range(NDT - 2, NDT):
                                w2 = half * NW + ws
                                t.wait_ge(xbr, w2 + 1)
                                t.wait_ge(sqr, w2 + 1)
                                sl = slice((ws % 2) * 512, (ws % 2 + 1) * 512)
                                t.matmul(pstt[0:1, 0:512], onescol_b[:, :],
                                         xb[:, sl], start=(ws == 0),
                                         stop=(ws == NDT - 1))
                                t.matmul(pstt[0:1, 512:1024], onescol_b[:, :],
                                         sq[:, sl], start=(ws == 0),
                                         stop=(ws == NDT - 1)).then_inc(stc, 1)

                    @blk.scalar
                    def _(s_, half=half):
                        bar.wait(s_)
                        if stat_out is not None:
                            for ot in range(NDT):
                                w = half * NW + ot
                                s_.wait_ge(dva, w + 1)
                                if w >= 2:
                                    s_.wait_ge(stc, w - 1)
                                sl = slice((ot % 2) * 512, (ot % 2 + 1) * 512)
                                s_.activation(sq[:, sl], xo[:, sl], AF.Square
                                              ).then_inc(sqr, 1)
                            s_.wait_ge(stc, (half + 1) * NW)
                            s_.activation(
                                stat_out[:1, 2 * half * 512:
                                         (2 * half + 1) * 512],
                                pstt[0:1, 0:512], AF.Copy, scale=1.0 / D
                            ).then_inc(fin, 1)
                            s_.activation(
                                stat_out[:1, (2 * half + 1) * 512:
                                         (2 * half + 2) * 512],
                                pstt[0:1, 512:1024], AF.Copy, scale=1.0 / D
                            ).then_inc(fin, 1)

                    @blk.gpsimd
                    def _(g_, half=half):
                        bar.wait(g_)
                        for ot in range(NDT):
                            w = half * NW + ot
                            g_.wait_ge(dva, w + 1)
                            g_.dma_start(
                                bass.AP(x2, ot * 128 * T + half * 512,
                                        [[T, 128], [1, 512]]),
                                xo[:, (ot % 2) * 512:(ot % 2 + 1) * 512],
                            ).then_inc(stxp[w % 2], 16)
                        g_.wait_ge(stxp[0], 16 * (half + 1) * (NW // 2))
                        g_.wait_ge(stxp[1], 16 * (half + 1) * (NW // 2))

            bar.close([(stxp[0], 16 * (half + 1) * (NW // 2)),
                       (stxp[1], 16 * (half + 1) * (NW // 2)),
                       (fin, 2 * (half + 1))])


def _phase_dbg(nc, sem, P, dram, dbg_t):
    with nc.sbuf_tensor("dbg_st", [128, 512], F32) as stg:
        dl = sem("dbg_l")
        ds_ = sem("dbg_s")
        with nc.Block() as blk:
            @blk.gpsimd
            def _(g_):
                P["bar"].wait(g_)
                cnt = 0
                for name, t in dram.items():
                    if name not in dbg_t:
                        continue
                    tap = dbg_t[name]
                    rows, cols = t.shape
                    for rt in range(rows // 128):
                        for ct in range(cols // 512):
                            g_.dma_start(
                                stg[:, :],
                                bass.AP(t, rt * 128 * cols + ct * 512,
                                        [[cols, 128], [1, 512]]),
                            ).then_inc(dl, 16)
                            cnt += 1
                            g_.wait_ge(dl, 16 * cnt)
                            g_.dma_start(
                                bass.AP(tap, rt * 128 * cols + ct * 512,
                                        [[cols, 128], [1, 512]]),
                                stg[:, :]).then_inc(ds_, 16)
                            g_.wait_ge(ds_, 16 * cnt)


_NC_CACHE = {}


def _get_program():
    key = tuple(DBG)
    if key not in _NC_CACHE:
        _NC_CACHE[key] = build_program()
    return _NC_CACHE[key]


def kernel(L0_attn_hidden_states, L0_attn_wq, L0_attn_wkv, L0_attn_wo,
           L0_ffn_W_base, L0_ffn_b_base, L0_ffn_enc0, L0_ffn_enc1,
           L0_ffn_coefs0, L0_ffn_coefs1, L0_ffn_mask0, L0_ffn_mask1,
           _collect_raw=False):
    x = np.asarray(L0_attn_hidden_states)[0]

    def _tile_w(W):
        nd, ne = W.shape[0] // 128, W.shape[1] // 128
        return np.ascontiguousarray(
            W.reshape(nd, 128, ne, 128).transpose(1, 2, 0, 3)
            .reshape(128, ne * nd * 128))

    wq_full = np.asarray(L0_attn_wq)
    wq_h = _tile_w(np.ascontiguousarray(
        wq_full.reshape(D, H, DQ)[:, :, :DN].reshape(D, H * DN)))
    wkv_full = np.asarray(L0_attn_wkv)
    wk_h = _tile_w(np.ascontiguousarray(wkv_full[:, :D]))
    wv_ = np.ascontiguousarray(wkv_full[:, D:2 * D])
    wo_h = _tile_w(np.asarray(L0_attn_wo))
    wbt_h = _tile_w(np.ascontiguousarray(np.asarray(L0_ffn_W_base).T))
    bb_ = np.ascontiguousarray(np.asarray(L0_ffn_b_base).reshape(1, D))
    enct_full = np.concatenate(
        [np.asarray(L0_ffn_enc0), np.asarray(L0_ffn_enc1)], 0).T
    enct_h = np.ascontiguousarray(
        enct_full.reshape(NDT, 128, RR).transpose(1, 0, 2).reshape(128, -1))
    cfp_ = np.empty((128, NDT, CF_W), dtype=np.float32)
    mkp_ = np.empty((128, NDT, CF_W), dtype=np.uint8)
    cfp_[:, :, :NE * R0] = np.asarray(L0_ffn_coefs0).reshape(
        NE, NDT, 128, R0).transpose(2, 1, 0, 3).reshape(128, NDT, NE * R0)
    cfp_[:, :, NE * R0:] = np.asarray(L0_ffn_coefs1).reshape(
        NE, NDT, 128, R1).transpose(2, 1, 0, 3).reshape(128, NDT, NE * R1)
    mkp_[:, :, :NE * R0] = np.asarray(L0_ffn_mask0).astype(np.uint8).reshape(
        NE, NDT, 128, R0).transpose(2, 1, 0, 3).reshape(128, NDT, NE * R0)
    mkp_[:, :, NE * R0:] = np.asarray(L0_ffn_mask1).astype(np.uint8).reshape(
        NE, NDT, 128, R1).transpose(2, 1, 0, 3).reshape(128, NDT, NE * R1)
    cfp_ = np.ascontiguousarray(cfp_.reshape(128, NDT * CF_W))
    mkp_ = np.ascontiguousarray(mkp_.reshape(128, NDT * CF_W))
    mtf_ = _host_fourier_mt()

    in_maps = []
    for c in range(NCORES):
        in_maps.append({
            "x0": np.ascontiguousarray(x[c::8, :].T),
            "wq": wq_h, "wk": wk_h, "wv": wv_, "wo": wo_h, "wbt": wbt_h,
            "bb": bb_, "enct": enct_h, "cfp": cfp_, "mkp": mkp_,
            "mtf": mtf_, "amask": _host_attn_masks(c),
        })
    nc = _get_program()
    res = run_bass_kernel_spmd(nc, in_maps, core_ids=list(range(NCORES)))
    if _collect_raw:
        return res
    full = np.empty((S, D), dtype=np.float32)
    for c in range(NCORES):
        full[c::8, :] = res.results[c]["out"].T
    return full.reshape(1, S, D)
